# version 16
# baseline (speedup 1.0000x reference)
import math
import numpy as np
from contextlib import ExitStack

import concourse.bass as bass
import concourse.mybir as mybir
from concourse.bass_utils import run_bass_kernel_spmd

F32 = mybir.dt.float32
BF16 = mybir.dt.bfloat16
AFT = mybir.ActivationFunctionType
AOP = mybir.AluOpType
AXX = mybir.AxisListType.X

D = 2048
CI = 16
T = 2048
TCH = 17
TLOC = TCH * 128
XCOL = 2049
NH = 16
OWN = 1024
NEGBIG = -1.0e30
QSCALE = 1.0 / math.sqrt(128.0)
EPS1 = 1e-5
EPS2 = 1e-6


class _Sem:
    def __init__(self, stack, nc, name):
        self.h = stack.enter_context(nc.semaphore(name))
        self.n = 0

    def inc(self, instr, k=1):
        instr.then_inc(self.h, k)
        self.n += k
        return self.n

    def wait(self, eng):
        eng.wait_ge(self.h, self.n)

    def wait_at(self, eng, v):
        if v > 0:
            eng.wait_ge(self.h, v)


class _ChSem(_Sem):

    def inc(self, instr, k=1):
        return self.n


class _Ch:

    def __init__(self, eng, sem):
        object.__setattr__(self, "_e", eng)
        object.__setattr__(self, "_s", sem)

    def wait_ge(self, h, v):
        self._e.wait_ge(h, v)

    def __getattr__(self, nm):
        f = getattr(self._e, nm)
        if nm.startswith("dma") or not callable(f):
            return f
        s = self._s
        e = self._e

        def g(*a, **k):
            if s.n:
                e.wait_ge(s.h, s.n)
            r = f(*a, **k)
            r.then_inc(s.h, 1)
            s.n += 1
            return r
        return g


def _build(debug=False):
    nc = bass.Bass("TRN2", target_bir_lowering=False)
    dt = nc.dram_tensor
    x0l = dt("x0l", [TLOC, D], F32, kind="ExternalInput").ap()
    wq_t = dt("wq_t", [3 * NH, CI, 128, 128], F32, kind="ExternalInput").ap()
    wout_t = dt("wout_t", [NH, CI, 128, 128], F32, kind="ExternalInput").ap()
    gate_t = dt("gate_t", [CI, CI, 128, 128], F32, kind="ExternalInput").ap()
    agg_t = dt("agg_t", [CI, 4, 128, 512], F32, kind="ExternalInput").ap()
    bc_t = dt("bc_t", [CI, 4, 128, 512], F32, kind="ExternalInput").ap()
    d1s_t = dt("d1s_t", [8, 4, 128, 512], F32, kind="ExternalInput").ap()
    d2s_t = dt("d2s_t", [8, 4, 128, 512], F32, kind="ExternalInput").ap()
    cosT_i = dt("cosT", [32, T + 1], F32, kind="ExternalInput").ap()
    sinT_i = dt("sinT", [32, T + 1], F32, kind="ExternalInput").ap()
    maskD_i = dt("maskD", [128, 128], F32, kind="ExternalInput").ap()
    maskO_i = dt("maskO", [128, 128], F32, kind="ExternalInput").ap()
    ident_i = dt("ident", [128, 128], F32, kind="ExternalInput").ap()
    ing_i = dt("ing_c", [128, CI], F32, kind="ExternalInput").ap()
    inb_i = dt("inb_c", [128, CI], F32, kind="ExternalInput").ap()
    og_i = dt("og_c", [128, CI], F32, kind="ExternalInput").ap()
    ob_i = dt("ob_c", [128, CI], F32, kind="ExternalInput").ap()
    gb_i = dt("gb_c", [128, CI], F32, kind="ExternalInput").ap()
    png_i = dt("png_r", [4, D], F32, kind="ExternalInput").ap()
    pnb_i = dt("pnb_r", [4, D], F32, kind="ExternalInput").ap()
    aggb_i = dt("aggb_r", [4, D], F32, kind="ExternalInput").ap()
    bcb_i = dt("bcb_r", [4, D], F32, kind="ExternalInput").ap()
    d1b_i = dt("d1b_r", [4, 1024], F32, kind="ExternalInput").ap()
    d2b_i = dt("d2b_r", [4, D], F32, kind="ExternalInput").ap()
    ohh_i = dt("ohh_r", [1, 4], F32, kind="ExternalInput").ap()
    oh1_i = dt("oh1_r", [4, 1], F32, kind="ExternalInput").ap()
    oT_sph = dt("oT_sph", [NH, 128, OWN + 1], BF16).ap()
    oT_spl = dt("oT_spl", [NH, 128, OWN + 1], BF16).ap()
    oT_tlh = dt("oT_tlh", [NH, 128, 1], BF16).ap()
    oT_tll = dt("oT_tll", [NH, 128, 1], BF16).ap()
    arin = dt("arin", [4, D], F32).ap()
    arout = dt("arout", [4, D], F32).ap()
    x1_sp = dt("x1_sp", [128, CI * OWN], F32).ap()
    y_out = dt("y_out", [OWN, D], F32, kind="ExternalOutput").ap()
    dbg = {}
    if debug:
        for nm, shp in [
            ("dbg_xln", [128, CI * XCOL]), ("dbg_xlnl", [128, CI * XCOL]),
            ("dbg_q", [128, OWN + 1]),
            ("dbg_k", [128, T]), ("dbg_v", [128, T]),
            ("dbg_ot", [128, OWN + 1]), ("dbg_x1", [128, CI * OWN]),
            ("dbg_h1", [128, CI * OWN]), ("dbg_gate", [128, CI * OWN]),
            ("dbg_th", [1, D]), ("dbg_bc", [128, CI]),
            ("dbg_ytail", [128, CI]), ("dbg_otbt", [128, NH]),
        ]:
            dty = (BF16 if nm in ("dbg_xln", "dbg_xlnl", "dbg_otbt")
                   else F32)
            dbg[nm] = dt(nm, shp, dty, kind="ExternalOutput").ap()

    st = ExitStack()
    with st:
        e = st.enter_context
        ident = e(nc.sbuf_tensor("identS", [128, 128], F32))
        onesf = e(nc.sbuf_tensor("onesf", [128, 1], F32))
        ones1r = e(nc.sbuf_tensor("ones1r", [1, 128], F32))
        onesb = e(nc.sbuf_tensor("onesb", [1, 128], BF16))
        epsc = e(nc.sbuf_tensor("epsc", [128, 2], F32))
        colc = {nm: e(nc.sbuf_tensor(nm + "_s", [128, CI], F32))
                for nm in ("ing", "inb", "og", "ob", "gb")}

        S = {}
        for nm in ("ldm", "ldx0a", "ldx0b", "ldwa", "ldwb", "ldwc", "ldo",
                   "ldx1a", "ldx1b", "spl", "pe", "out", "gq", "cc", "ccd"):
            S[nm] = _Sem(st, nc, "s_" + nm)
        S["dbgs"] = _Sem(st, nc, "s_dbgs")
        for nm in ("dve", "sc"):
            S[nm] = _ChSem(st, nc, "s_" + nm)
        ldm = S["ldm"]
        ldx0 = [S["ldx0a"], S["ldx0b"]]
        ldw = [S["ldwa"], S["ldwb"]]
        ldo = S["ldo"]
        ldx1 = [S["ldx1a"], S["ldx1b"]]
        spl, pe, dve, sc, out_s, gq = (S["spl"], S["pe"], S["dve"], S["sc"],
                                       S["out"], S["gq"])
        dbgs = S["dbgs"]

        blk = e(nc.Block())

        @blk.sync
        def _(q):
            for dst, src in [(ident, ident_i),
                             (colc["ing"], ing_i), (colc["inb"], inb_i),
                             (colc["og"], og_i), (colc["ob"], ob_i),
                             (colc["gb"], gb_i)]:
                ldm.inc(q.dma_start(out=dst[:], in_=src), 16)

        @blk.vector
        def _(v):
            v = _Ch(v, dve)
            v.memset(onesf[:], 1.0)
            v.memset(ones1r[:], 1.0)
            v.memset(epsc[:, 0:1], EPS1)
            v.memset(epsc[:, 1:2], EPS2)
            dve.inc(v.memset(onesb[:], 1.0))

        with ExitStack() as p12:
            xlnTh = p12.enter_context(
                nc.sbuf_tensor("xlnTh", [128, CI * XCOL], BF16))
            xlnTl = p12.enter_context(
                nc.sbuf_tensor("xlnTl", [128, CI * XCOL], BF16))

            def xlh(ci, a, b):
                return xlnTh[:, ci * XCOL + a: ci * XCOL + b]

            def xll(ci, a, b):
                return xlnTl[:, ci * XCOL + a: ci * XCOL + b]

            with ExitStack() as ph:
                pe_ = ph.enter_context
                x0tm = [pe_(nc.sbuf_tensor(f"x0tm{i}", [128, D], F32))
                        for i in range(2)]
                xlntm = [pe_(nc.sbuf_tensor(f"xlntm{i}", [128, D], F32))
                         for i in range(2)]
                sqscr = pe_(nc.sbuf_tensor("sqscr", [128, D], F32))
                stat = pe_(nc.sbuf_tensor("stat", [128, 16], F32))
                psT = [pe_(nc.psum_tensor(f"psT{i}", [128, 512], F32))
                       for i in range(2)]

                norm_done = [0, 0]
                trs_done = [0, 0]
                psT_ev = [0, 0]

                for tc in range(TCH):
                    s = tc % 2
                    so = s * 8

                    @blk.sync
                    def _(q, tc=tc, s=s):
                        if tc >= 2:
                            dve.wait_at(q, norm_done[s])
                        ldx0[s].inc(q.dma_start(
                            out=x0tm[s][:],
                            in_=x0l[tc * 128:(tc + 1) * 128, :]), 16)
                    ld_v = ldx0[s].n

                    @blk.scalar
                    def _(c, s=s, so=so, ld_v=ld_v):
                        c = _Ch(c, sc)
                        ldx0[s].wait_at(c, ld_v)
                        sc.inc(c.activation(sqscr[:], x0tm[s][:], AFT.Square,
                                            accum_out=stat[:, so + 1:so + 2]))
                    sq_v = sc.n

                    @blk.vector
                    def _(v, s=s, so=so, ld_v=ld_v, sq_v=sq_v):
                        v = _Ch(v, dve)
                        ldx0[s].wait_at(v, ld_v)
                        v.reduce_sum(stat[:, so:so + 1], x0tm[s][:], axis=AXX)
                        v.tensor_scalar_mul(stat[:, so + 2:so + 3],
                                            stat[:, so:so + 1], 1.0 / D)
                        v.tensor_mul(stat[:, so + 3:so + 4],
                                     stat[:, so + 2:so + 3],
                                     stat[:, so + 2:so + 3])
                        sc.wait_at(v, sq_v)
                        v.tensor_scalar_mul(stat[:, so + 4:so + 5],
                                            stat[:, so + 1:so + 2], 1.0 / D)
                        dve.inc(v.tensor_sub(stat[:, so + 5:so + 6],
                                             stat[:, so + 4:so + 5],
                                             stat[:, so + 3:so + 4]))
                    var_v = dve.n

                    @blk.scalar
                    def _(c, so=so, var_v=var_v):
                        c = _Ch(c, sc)
                        dve.wait_at(c, var_v)
                        sc.inc(c.activation(stat[:, so + 6:so + 7],
                                            stat[:, so + 5:so + 6], AFT.Sqrt,
                                            bias=epsc[:, 0:1]))
                    sd_v = sc.n

                    @blk.vector
                    def _(v, s=s, so=so, sd_v=sd_v, tc=tc):
                        v = _Ch(v, dve)
                        sc.wait_at(v, sd_v)
                        v.reciprocal(stat[:, so + 7:so + 8],
                                     stat[:, so + 6:so + 7])
                        if tc >= 2:
                            pe.wait_at(v, trs_done[s])
                        dve.inc(v.tensor_scalar(xlntm[s][:], x0tm[s][:],
                                                stat[:, so + 2:so + 3],
                                                stat[:, so + 7:so + 8],
                                                AOP.subtract, AOP.mult))
                    norm_done[s] = dve.n

                    for g in range(4):
                        pb = (tc * 4 + g) % 2

                        @blk.tensor
                        def _(t, g=g, pb=pb, s=s, tc=tc):
                            if g == 0:
                                dve.wait_at(t, norm_done[s])
                                ldm.wait(t)
                            dve.wait_at(t, psT_ev[pb])
                            for i4 in range(4):
                                ci = g * 4 + i4
                                mm = t.matmul(
                                    psT[pb][:, i4 * 128:(i4 + 1) * 128],
                                    xlntm[s][:, ci * 128:(ci + 1) * 128],
                                    ident[:], is_transpose=True,
                                    start=(i4 == 0), stop=(i4 == 3))
                            pe.inc(mm)
                        gv = pe.n

                        @blk.scalar
                        def _(c, g=g, pb=pb, tc=tc, gv=gv):
                            c = _Ch(c, sc)
                            pe.wait_at(c, gv)
                            rh = xlnTh[:].rearrange("p (ci t) -> p ci t", ci=CI)
                            s_ = psT[pb][:].rearrange("p (a b) -> p a b", a=4)
                            if tc < 16:
                                sc.inc(c.copy(rh[:, g * 4:(g + 1) * 4,
                                                 tc * 128:(tc + 1) * 128], s_))
                            else:
                                sc.inc(c.copy(rh[:, g * 4:(g + 1) * 4,
                                                 2048:2049],
                                              s_[:, :, 127:128]))
                        hi_v = sc.n

                        @blk.vector
                        def _(v, g=g, pb=pb, tc=tc, hi_v=hi_v):
                            v = _Ch(v, dve)
                            sc.wait_at(v, hi_v)
                            rh = xlnTh[:].rearrange("p (ci t) -> p ci t", ci=CI)
                            rl = xlnTl[:].rearrange("p (ci t) -> p ci t", ci=CI)
                            s_ = psT[pb][:].rearrange("p (a b) -> p a b", a=4)
                            if tc < 16:
                                sl_h = rh[:, g * 4:(g + 1) * 4,
                                          tc * 128:(tc + 1) * 128]
                                sl_l = rl[:, g * 4:(g + 1) * 4,
                                          tc * 128:(tc + 1) * 128]
                                dve.inc(v.tensor_sub(sl_l, s_, sl_h))
                            else:
                                dve.inc(v.tensor_sub(
                                    rl[:, g * 4:(g + 1) * 4, 2048:2049],
                                    s_[:, :, 127:128],
                                    rh[:, g * 4:(g + 1) * 4, 2048:2049]))
                        psT_ev[pb] = dve.n
                    trs_done[s] = pe.n
                entry_ev = sc.n
                if debug:
                    @blk.gpsimd
                    def _(g):
                        sc.wait_at(g, entry_ev)
                        dve.wait_at(g, psT_ev[1])
                        dbgs.inc(g.dma_start(out=dbg["dbg_xln"],
                                              in_=xlnTh[:]), 16)
                        dbgs.inc(g.dma_start(out=dbg["dbg_xlnl"],
                                              in_=xlnTl[:]), 16)

            with ExitStack() as ph:
                pe_ = ph.enter_context
                cosT = pe_(nc.sbuf_tensor("cosT_s", [32, T + 1], F32))
                sinT = pe_(nc.sbuf_tensor("sinT_s", [32, T + 1], F32))
                maskD = pe_(nc.sbuf_tensor("maskD_s", [128, 128], F32))
                maskO = pe_(nc.sbuf_tensor("maskO_s", [128, 128], F32))
                wstg = pe_(nc.sbuf_tensor("wstg", [128, 4 * 128], F32))
                wph = [pe_(nc.sbuf_tensor(f"wph{i}", [128, 4 * 128], BF16))
                       for i in range(2)]
                wpl = [pe_(nc.sbuf_tensor(f"wpl{i}", [128, 4 * 128], BF16))
                       for i in range(2)]
                Kh = pe_(nc.sbuf_tensor("Kh", [128, T], F32))
                Vh = pe_(nc.sbuf_tensor("Vh", [128, T], F32))
                Qh = pe_(nc.sbuf_tensor("Qh", [128, OWN + 1], F32))
                scr = pe_(nc.sbuf_tensor("scr", [128, T], F32))
                PT = [pe_(nc.sbuf_tensor(f"PT{i}", [128, T], F32))
                      for i in range(2)]
                PTt = pe_(nc.sbuf_tensor("PTt", [128, 16], F32))
                OTh = pe_(nc.sbuf_tensor("OTh", [128, OWN + 1], F32))
                cvec = pe_(nc.sbuf_tensor("cvec", [1, OWN + 1], F32))
                negcb = scr[0:1, 1040:1553].bitcast(BF16)[:, 0:OWN + 1]
                nct = pe_(nc.sbuf_tensor("nct", [128, 1], F32))
                invs = cvec[0:1, 0:OWN]
                kqv = pe_(nc.sbuf_tensor("kqv", [1, 8], F32))
                psST = pe_(nc.psum_tensor("psST", [128, 2048], F32))
                psO = [pe_(nc.psum_tensor(f"psO{i}", [128, 128], F32))
                       for i in range(2)]
                psSum = pe_(nc.psum_tensor("psSum", [1, OWN], F32))

                @blk.sync
                def _(q):
                    ldx0[0].wait(q)
                    ldx0[1].wait(q)
                    dve.wait(q)
                    sc.wait(q)
                    pe.wait(q)
                    for dst, src in [(cosT, cosT_i), (sinT, sinT_i),
                                     (maskD, maskD_i), (maskO, maskO_i)]:
                        ldm.inc(q.dma_start(out=dst[:], in_=src), 16)
                ldm2 = ldm.n

                wld = {}
                wsplit = {}
                wconsumed = {}

                def load_half(q, hidx):
                    pidx, qq = divmod(hidx, 4)
                    src = wq_t[pidx, qq * 4:(qq + 1) * 4].transpose([1, 0, 2])
                    if hidx >= 1:
                        dve.wait_at(q, wsplit[hidx - 1])
                    ldw[hidx % 2].inc(
                        q.dma_start(
                            out=wstg[:].rearrange("p (ci f) -> p ci f", ci=4),
                            in_=src), 16)
                    wld[hidx] = ldw[hidx % 2].n

                def split_half(hidx):
                    @blk.scalar
                    def _(c, hidx=hidx):
                        c = _Ch(c, sc)
                        ldw[hidx % 2].wait_at(c, wld[hidx])
                        if hidx >= 2:
                            pe.wait_at(c, wconsumed[hidx - 2])
                        sc.inc(c.copy(wph[hidx % 2][:], wstg[:]))
                    cast_v = sc.n

                    @blk.vector
                    def _(v, hidx=hidx, cast_v=cast_v):
                        v = _Ch(v, dve)
                        sc.wait_at(v, cast_v)
                        dve.inc(v.tensor_sub(wpl[hidx % 2][:], wstg[:],
                                             wph[hidx % 2][:]))
                    wsplit[hidx] = dve.n

                @blk.sync
                def _(q):
                    load_half(q, 0)
                split_half(0)

                @blk.sync
                def _(q):
                    load_half(q, 1)
                split_half(1)

                @blk.sync
                def _(q):
                    load_half(q, 2)

                prev = dict(invrep_ev=0, spl=0, onorm=0, invs=0,
                            pv6=0, pv7=0, rope=0, tpv=0)

                QSL = ((0, 512), (512, 512), (2048, 1))

                NQT = 12 * NH
                for h in range(NH):
                    for hh in range(4):
                        hidx = 12 * h + hh

                        @blk.tensor
                        def _(t, h=h, hh=hh, hidx=hidx):
                            dve.wait_at(t, wsplit[hidx])
                            if hh == 0:
                                dve.wait_at(t, psT_ev[1] if not h else 0)
                                sc.wait_at(t, prev["invrep_ev"] if h
                                           else entry_ev)
                            for c4 in range(4):
                                ci = hh * 4 + c4
                                wbh = wph[hidx % 2][:, c4 * 128:(c4 + 1) * 128]
                                wbl = wpl[hidx % 2][:, c4 * 128:(c4 + 1) * 128]
                                for sl, (t0, tw) in enumerate(QSL):
                                    reg = psST[:, sl * 512:sl * 512 + tw]
                                    t.matmul(reg, wbh, xlh(ci, t0, t0 + tw),
                                             start=(ci == 0), stop=False)
                                    t.matmul(reg, wbh, xll(ci, t0, t0 + tw),
                                             start=False, stop=False)
                                    mm = t.matmul(
                                        reg, wbl, xlh(ci, t0, t0 + tw),
                                        start=False, stop=(ci == CI - 1))
                            pe.inc(mm)
                        wconsumed[hidx] = pe.n

                        if hidx + 2 < NQT:
                            split_half(hidx + 2)

                        @blk.sync
                        def _(q, hidx=hidx):
                            if hidx + 3 < NQT:
                                load_half(q, hidx + 3)
                    q_mm = pe.n

                    @blk.scalar
                    def _(c, q_mm=q_mm, h=h):
                        c = _Ch(c, sc)
                        pe.wait_at(c, q_mm)
                        if debug and h == 1:
                            dbgs.wait_at(c, dbg0_v)
                        for sl, (t0, tw) in enumerate(((0, 512), (512, 512),
                                                       (1024, 1))):
                            sc.inc(c.activation(
                                Qh[:, t0:t0 + tw],
                                psST[:, sl * 512:sl * 512 + tw],
                                AFT.Copy, scale=QSCALE))
                    q_ev = sc.n

                    for hh in range(4):
                        hidx = 12 * h + 4 + hh

                        @blk.tensor
                        def _(t, hh=hh, hidx=hidx, q_ev=q_ev):
                            dve.wait_at(t, wsplit[hidx])
                            if hh == 0:
                                sc.wait_at(t, q_ev)
                            for c4 in range(4):
                                ci = hh * 4 + c4
                                wbh = wph[hidx % 2][:, c4 * 128:(c4 + 1) * 128]
                                wbl = wpl[hidx % 2][:, c4 * 128:(c4 + 1) * 128]
                                for sl in range(4):
                                    reg = psST[:, sl * 512:(sl + 1) * 512]
                                    a0, a1 = sl * 512, (sl + 1) * 512
                                    t.matmul(reg, wbh, xlh(ci, a0, a1),
                                             start=(ci == 0), stop=False)
                                    t.matmul(reg, wbh, xll(ci, a0, a1),
                                             start=False, stop=False)
                                    mm = t.matmul(
                                        reg, wbl, xlh(ci, a0, a1),
                                        start=False, stop=(ci == CI - 1))
                            pe.inc(mm)
                        wconsumed[hidx] = pe.n

                        if hidx + 2 < NQT:
                            split_half(hidx + 2)

                        @blk.sync
                        def _(q, hidx=hidx):
                            if hidx + 3 < NQT:
                                load_half(q, hidx + 3)
                    k_mm = pe.n

                    @blk.scalar
                    def _(c, k_mm=k_mm):
                        c = _Ch(c, sc)
                        pe.wait_at(c, k_mm)
                        for sl in range(4):
                            sc.inc(c.copy(Kh[:, sl * 512:(sl + 1) * 512],
                                          psST[:, sl * 512:(sl + 1) * 512]))
                    k_ev = sc.n

                    ksw = PT[0][0:32, 0:T]
                    qsw = PT[1][0:32, 0:OWN + 1]

                    @blk.scalar
                    def _(c, h=h, q_ev=q_ev):
                        c = _Ch(c, sc)
                        sc.wait_at(c, sc.n)
                        ldm.wait_at(c, ldm2)
                        dve.wait_at(c, prev["onorm"])
                        pe.wait_at(c, prev["pv7"])
                        gq.inc(c.dma_start(out=qsw[0:16, :],
                                           in_=Qh[16:32, 0:OWN + 1]), 16)
                        gq.inc(c.dma_start(out=qsw[16:32, :],
                                           in_=Qh[0:16, 0:OWN + 1]), 16)
                        gq.inc(c.dma_start(out=ksw[0:16, :],
                                           in_=Kh[16:32, 0:T]), 16)
                        gq.inc(c.dma_start(out=ksw[16:32, :],
                                           in_=Kh[0:16, 0:T]), 16)
                    sw_v = gq.n

                    @blk.vector
                    def _(v, sw_v=sw_v):
                        v = _Ch(v, dve)
                        gq.wait_at(v, sw_v)
                        v.tensor_mul(Qh[0:32, 0:OWN], Qh[0:32, 0:OWN],
                                     cosT[:, 0:OWN])
                        v.tensor_mul(qsw[:, 0:OWN], qsw[:, 0:OWN],
                                     sinT[:, 0:OWN])
                        v.tensor_add(Qh[0:32, 0:OWN], Qh[0:32, 0:OWN],
                                     qsw[:, 0:OWN])
                        v.tensor_mul(Qh[0:32, OWN:OWN + 1],
                                     Qh[0:32, OWN:OWN + 1], cosT[:, T:T + 1])
                        v.tensor_mul(qsw[:, OWN:OWN + 1], qsw[:, OWN:OWN + 1],
                                     sinT[:, T:T + 1])
                        v.tensor_add(Qh[0:32, OWN:OWN + 1],
                                     Qh[0:32, OWN:OWN + 1],
                                     qsw[:, OWN:OWN + 1])
                        v.tensor_mul(Kh[0:32, :], Kh[0:32, :], cosT[:, 0:T])
                        v.tensor_mul(ksw[:], ksw[:], sinT[:, 0:T])
                        dve.inc(v.tensor_add(Kh[0:32, :], Kh[0:32, :],
                                             ksw[:]))
                    prev["rope"] = rope_v = dve.n

                    @blk.scalar
                    def _(c, rope_v=rope_v):
                        c = _Ch(c, sc)
                        dve.wait_at(c, rope_v)
                        sc.inc(c.activation(PT[1][:], Kh[:], AFT.Square))
                    ksq_v = sc.n

                    for hh in range(4):
                        hidx = 12 * h + 8 + hh

                        @blk.tensor
                        def _(t, hh=hh, hidx=hidx, k_ev=k_ev):
                            dve.wait_at(t, wsplit[hidx])
                            if hh == 0:
                                sc.wait_at(t, k_ev)
                            for c4 in range(4):
                                ci = hh * 4 + c4
                                wbh = wph[hidx % 2][:, c4 * 128:(c4 + 1) * 128]
                                wbl = wpl[hidx % 2][:, c4 * 128:(c4 + 1) * 128]
                                for sl in range(4):
                                    reg = psST[:, sl * 512:(sl + 1) * 512]
                                    a0, a1 = sl * 512, (sl + 1) * 512
                                    t.matmul(reg, wbh, xlh(ci, a0, a1),
                                             start=(ci == 0), stop=False)
                                    t.matmul(reg, wbh, xll(ci, a0, a1),
                                             start=False, stop=False)
                                    mm = t.matmul(
                                        reg, wbl, xlh(ci, a0, a1),
                                        start=False, stop=(ci == CI - 1))
                            pe.inc(mm)
                        wconsumed[hidx] = pe.n

                        if hidx + 2 < NQT:
                            split_half(hidx + 2)

                        @blk.sync
                        def _(q, hidx=hidx):
                            if hidx + 3 < NQT:
                                load_half(q, hidx + 3)
                    v_mm = pe.n

                    @blk.scalar
                    def _(c, v_mm=v_mm):
                        c = _Ch(c, sc)
                        pe.wait_at(c, v_mm)
                        dve.wait_at(c, prev["onorm"])
                        for sl in range(4):
                            sc.inc(c.copy(scr[:, sl * 512:(sl + 1) * 512],
                                          psST[:, sl * 512:(sl + 1) * 512]))
                    vd_ev = sc.n

                    @blk.tensor
                    def _(t, ksq_v=ksq_v):
                        sc.wait_at(t, ksq_v)
                        t.matmul(psSum[0:1, 0:512], onesf[:],
                                 PT[1][:, 0:512], start=True, stop=True)
                        pe.inc(t.matmul(psSum[0:1, 512:1024], onesf[:],
                                        PT[1][:, 512:1024],
                                        start=True, stop=True))
                    ksumA_v = pe.n

                    @blk.vector
                    def _(v, ksumA_v=ksumA_v):
                        v = _Ch(v, dve)
                        pe.wait_at(v, ksumA_v)
                        dve.inc(v.reduce_max(kqv[:, 2:3], psSum[0:1, 0:1024],
                                             axis=AXX))
                    kmaxA_v = dve.n

                    v_ev = [0] * 4
                    for tg in range(4):
                        @blk.tensor
                        def _(t, tg=tg, vd_ev=vd_ev):
                            if tg == 0:
                                sc.wait_at(t, vd_ev)
                            else:
                                sc.wait_at(t, v_ev[tg - 1])
                            for t4 in range(4):
                                tc = tg * 4 + t4
                                mm = t.matmul(
                                    psST[:, t4 * 128:(t4 + 1) * 128],
                                    scr[:, tc * 128:(tc + 1) * 128],
                                    ident[:], is_transpose=True,
                                    start=(t4 == 0), stop=(t4 == 3))
                            pe.inc(mm)
                        vtr = pe.n

                        @blk.scalar
                        def _(c, tg=tg, vtr=vtr):
                            c = _Ch(c, sc)
                            pe.wait_at(c, vtr)
                            if tg == 0:
                                spl.wait_at(c, prev["spl"])
                            sc.inc(c.copy(
                                Vh[:].rearrange("p (a b) -> p a b", a=16)
                                [:, tg * 4:(tg + 1) * 4, :],
                                psST[:, 0:512]
                                .rearrange("p (a b) -> p a b", a=4)))
                        v_ev[tg] = sc.n



                    @blk.tensor
                    def _(t, kmaxA_v=kmaxA_v):
                        dve.wait_at(t, kmaxA_v)
                        t.matmul(psSum[0:1, 0:512], onesf[:],
                                 PT[1][:, 1024:1536], start=True, stop=True)
                        pe.inc(t.matmul(psSum[0:1, 512:1024], onesf[:],
                                        PT[1][:, 1536:2048],
                                        start=True, stop=True))
                    ksumB_v = pe.n

                    @blk.vector
                    def _(v, ksumB_v=ksumB_v):
                        v = _Ch(v, dve)
                        pe.wait_at(v, ksumB_v)
                        v.reduce_max(kqv[:, 3:4], psSum[0:1, 0:1024],
                                     axis=AXX)
                        dve.inc(v.tensor_tensor(kqv[:, 0:1], kqv[:, 2:3],
                                                kqv[:, 3:4], AOP.max))
                    kmaxB_v = dve.n

                    @blk.scalar
                    def _(c, h=h, ksumA_v=ksumA_v):
                        c = _Ch(c, sc)
                        pe.wait_at(c, ksumA_v)
                        if h:
                            pe.wait_at(c, prev["tpv"])
                        c.activation(PT[1][:, 0:1024], Qh[:, 0:1024],
                                     AFT.Square)
                        sc.inc(c.activation(PTt[:, 0:1], Qh[:, OWN:OWN + 1],
                                            AFT.Square))
                    qsq_v = sc.n

                    @blk.tensor
                    def _(t, qsq_v=qsq_v, kmaxB_v=kmaxB_v):
                        sc.wait_at(t, qsq_v)
                        dve.wait_at(t, kmaxB_v)
                        t.matmul(psSum[0:1, 0:512], onesf[:],
                                 PT[1][:, 0:512], start=True, stop=True)
                        t.matmul(psSum[0:1, 512:1024], onesf[:],
                                 PT[1][:, 512:1024], start=True, stop=True)
                        pe.inc(t.matmul(psO[0][0:1, 0:1], onesf[:],
                                        PTt[:, 0:1], start=True, stop=True))
                    qsum_v = pe.n

                    @blk.vector
                    def _(v, qsum_v=qsum_v):
                        v = _Ch(v, dve)
                        pe.wait_at(v, qsum_v)
                        v.tensor_single_scalar(cvec[0:1, 0:OWN],
                                               psSum[0:1, 0:OWN],
                                               kqv[:, 0:1], AOP.mult)
                        dve.inc(v.tensor_single_scalar(
                            cvec[0:1, OWN:OWN + 1], psO[0][0:1, 0:1],
                            kqv[:, 0:1], AOP.mult))
                    c2_v = dve.n

                    @blk.scalar
                    def _(c, c2_v=c2_v):
                        c = _Ch(c, sc)
                        dve.wait_at(c, c2_v)
                        sc.inc(c.activation(cvec[:], cvec[:], AFT.Sqrt))
                    c_v = sc.n

                    @blk.vector
                    def _(v, c_v=c_v):
                        v = _Ch(v, dve)
                        sc.wait_at(v, c_v)
                        dve.inc(v.tensor_scalar_mul(negcb, cvec[:], -1.0))
                    negc_v = dve.n

                    exp_v = {}
                    exp_vb = {}
                    pv_v = {}
                    pso_ev = {}
                    for j in range(9):
                        if j < 8:
                            kcs = list(range(j + 1)) + list(range(8, 9 + j))
                            nb = len(kcs)

                            @blk.tensor
                            def _(t, j=j, kcs=kcs, nb=nb,
                                  rope_v=rope_v, negc_v=negc_v):
                                if j == 0:
                                    dve.wait_at(t, rope_v)
                                    sc.wait_at(t, v_ev[3])
                                for bi, kc in enumerate(kcs):
                                    if j >= 1 and bi % 4 == 0:
                                        pvb = exp_vb[j - 1]
                                        b = bi // 4
                                        if b < len(pvb):
                                            sc.wait_at(t, pvb[b])
                                    t.matmul(
                                        psST[:, bi * 128:(bi + 1) * 128],
                                        Kh[:, kc * 128:(kc + 1) * 128],
                                        Qh[:, j * 128:(j + 1) * 128],
                                        start=(bi % 4 == 0), stop=False)
                                if j == 0:
                                    dve.wait_at(t, negc_v)
                                for bi in range(nb):
                                    mm = t.matmul(
                                        psST[:, bi * 128:(bi + 1) * 128],
                                        onesb[:],
                                        negcb[0:1, j * 128:(j + 1) * 128],
                                        start=False,
                                        stop=(bi % 4 == 3 or bi == nb - 1))
                                pe.inc(mm)
                            st_v = pe.n

                            @blk.vector
                            def _(v, j=j, nb=nb, st_v=st_v):
                                v = _Ch(v, dve)
                                pe.wait_at(v, st_v)
                                v.tensor_tensor(
                                    psST[:, j * 128:(j + 1) * 128],
                                    psST[:, j * 128:(j + 1) * 128],
                                    maskD[:], AOP.add)
                                dve.inc(v.tensor_tensor(
                                    psST[:, (nb - 1) * 128:nb * 128],
                                    psST[:, (nb - 1) * 128:nb * 128],
                                    maskO[:], AOP.add))
                            mask_v = dve.n

                            @blk.scalar
                            def _(c, j=j, nb=nb, mask_v=mask_v):
                                c = _Ch(c, sc)
                                dve.wait_at(c, mask_v)
                                if j >= 2:
                                    pe.wait_at(c, pv_v[j - 2])
                                elif j == 0:
                                    pe.wait_at(c, prev["pv6"])
                                else:
                                    pe.wait_at(c, prev["pv7"])
                                sc.inc(c.activation(PT[j % 2][:, 0:nb * 128],
                                                    psST[:, 0:nb * 128],
                                                    AFT.Exp))
                                exp_vb[j] = [sc.n]
                            exp_v[j] = sc.n

                        if j == 8:
                            @blk.tensor
                            def _(t):
                                sc.wait_at(t, pso_ev[6])
                                for kc in range(16):
                                    mm = t.matmul(
                                        psO[0][:, 1 + kc:2 + kc],
                                        Kh[:, kc * 128:(kc + 1) * 128],
                                        Qh[:, OWN:OWN + 1],
                                        start=(kc == 0), stop=(kc == 15))
                                pe.inc(mm)
                            tst_v = pe.n

                        if j >= 1:
                            jj = j - 1
                            kcs2 = list(range(jj + 1)) + list(range(8, 9 + jj))
                            nb2 = len(kcs2)

                            @blk.tensor
                            def _(t, h=h, jj=jj, kcs2=kcs2, nb2=nb2):
                                sc.wait_at(t, exp_v[jj])
                                sc.wait_at(t, pso_ev.get(jj - 2, 0))
                                if jj == 0 and h:
                                    dve.wait_at(t, prev["invs"])
                                s_start = jj in (0, 4)
                                s_stop = jj in (3, 7)
                                for bi in range(nb2):
                                    t.matmul(
                                        psO[jj % 2][:, 0:128],
                                        Vh[:, kcs2[bi] * 128:
                                           (kcs2[bi] + 1) * 128],
                                        PT[jj % 2][:, bi * 128:(bi + 1) * 128],
                                        start=(bi == 0), stop=(bi == nb2 - 1))
                                for bi in range(nb2):
                                    mm = t.matmul(
                                        psSum[0:1, jj * 128:(jj + 1) * 128],
                                        onesf[:],
                                        PT[jj % 2][:, bi * 128:(bi + 1) * 128],
                                        start=(s_start and bi == 0),
                                        stop=(s_stop and bi == nb2 - 1))
                                pe.inc(mm)
                            pv_v[jj] = pe.n

                            @blk.scalar
                            def _(c, h=h, jj=jj):
                                c = _Ch(c, sc)
                                pe.wait_at(c, pv_v[jj])
                                if jj == 0 and h:
                                    spl.wait_at(c, prev["spl"])
                                sc.inc(c.copy(OTh[:, jj * 128:(jj + 1) * 128],
                                              psO[jj % 2][:]))
                            pso_ev[jj] = sc.n

                        if j == 1:
                            @blk.tensor
                            def _(t, negc_v=negc_v):
                                dve.wait_at(t, negc_v)
                                pe.inc(t.matmul(psST[:, 1920:1921], onesb[:],
                                                negcb[0:1, OWN:OWN + 1],
                                                start=True, stop=True))
                            nctp_v = pe.n

                            @blk.scalar
                            def _(c, nctp_v=nctp_v):
                                c = _Ch(c, sc)
                                pe.wait_at(c, nctp_v)
                                sc.inc(c.copy(nct[:], psST[:, 1920:1921]))
                            nct_v = sc.n
                    prev["pv6"] = pv_v[6]
                    prev["pv7"] = pv_v[7]

                    @blk.scalar
                    def _(c, tst_v=tst_v):
                        c = _Ch(c, sc)
                        pe.wait_at(c, tst_v)
                        sc.inc(c.activation(PTt[:], psO[0][:, 1:17], AFT.Exp,
                                            bias=nct[:]))
                    texp_v = sc.n

                    @blk.tensor
                    def _(t, texp_v=texp_v):
                        sc.wait_at(t, texp_v)
                        sc.wait_at(t, pso_ev[7])
                        for kc in range(16):
                            t.matmul(psO[0][:, 0:1],
                                     Vh[:, kc * 128:(kc + 1) * 128],
                                     PTt[:, kc:kc + 1],
                                     start=(kc == 0), stop=(kc == 15))
                        for kc in range(16):
                            mm2 = t.matmul(psO[1][0:1, 0:1], onesf[:],
                                           PTt[:, kc:kc + 1],
                                           start=(kc == 0), stop=(kc == 15))
                        pe.inc(mm2)
                    tpv_v = pe.n
                    prev["tpv"] = tpv_v

                    @blk.vector
                    def _(v, tpv_v=tpv_v):
                        v = _Ch(v, dve)
                        pe.wait_at(v, tpv_v)
                        v.reciprocal(invs, psSum[0:1, :])
                        dve.inc(v.reciprocal(kqv[:, 1:2], psO[1][0:1, 0:1]))
                    prev["invs"] = invs_v = dve.n

                    @blk.tensor
                    def _(t, invs_v=invs_v):
                        dve.wait_at(t, invs_v)
                        for sl in range(2):
                            t.matmul(psST[:, sl * 512:(sl + 1) * 512],
                                     ones1r[:],
                                     invs[0:1, sl * 512:(sl + 1) * 512]
                                     if False else
                                     cvec[0:1, sl * 512:(sl + 1) * 512],
                                     start=True, stop=True)
                        mm = t.matmul(psST[:, 1024:1025], ones1r[:],
                                      kqv[0:1, 1:2], start=True, stop=True)
                        pe.inc(mm)
                    irep_v = pe.n

                    @blk.scalar
                    def _(c, irep_v=irep_v):
                        c = _Ch(c, sc)
                        pe.wait_at(c, irep_v)
                        sc.inc(c.copy(scr[:, 0:1025], psST[:, 0:1025]))
                    prev["invrep_ev"] = irep_ev = sc.n

                    @blk.vector
                    def _(v, irep_ev=irep_ev):
                        v = _Ch(v, dve)
                        sc.wait_at(v, irep_ev)
                        v.tensor_tensor(OTh[:, 0:OWN], OTh[:, 0:OWN],
                                        scr[:, 0:OWN], AOP.mult)
                        dve.inc(v.tensor_tensor(OTh[:, OWN:OWN + 1],
                                                psO[0][:, 0:1],
                                                scr[:, OWN:OWN + 1], AOP.mult))
                    prev["onorm"] = onorm_v = dve.n

                    if debug and h == 0:
                        @blk.gpsimd
                        def _(g, onorm_v=onorm_v):
                            dve.wait_at(g, onorm_v)
                            dbgs.inc(g.dma_start(out=dbg["dbg_q"],
                                                  in_=Qh[:]), 16)
                            dbgs.inc(g.dma_start(out=dbg["dbg_k"],
                                                  in_=Kh[:]), 16)
                            dbgs.inc(g.dma_start(out=dbg["dbg_v"],
                                                  in_=Vh[:]), 16)
                            dbgs.inc(g.dma_start(out=dbg["dbg_ot"],
                                                  in_=OTh[:]), 16)
                        dbg0_v = dbgs.n
                    OThh = Vh[:, 0:513].bitcast(BF16)[:, 0:OWN + 1]
                    OThl = Vh[:, 1024:1537].bitcast(BF16)[:, 0:OWN + 1]

                    @blk.scalar
                    def _(c, h=h, onorm_v=onorm_v):
                        c = _Ch(c, sc)
                        dve.wait_at(c, onorm_v)
                        if debug and h == 0:
                            dbgs.wait_at(c, dbg0_v)
                        sc.inc(c.copy(OThh, OTh[:]))
                    othi_v = sc.n

                    @blk.vector
                    def _(v, othi_v=othi_v):
                        v = _Ch(v, dve)
                        sc.wait_at(v, othi_v)
                        dve.inc(v.tensor_sub(OThl, OTh[:], OThh))
                    otlo_v = dve.n

                    @blk.gpsimd
                    def _(g, h=h, otlo_v=otlo_v):
                        dve.wait_at(g, otlo_v)
                        spl.inc(g.dma_start(out=oT_sph[h], in_=OThh), 16)
                        spl.inc(g.dma_start(out=oT_spl[h], in_=OThl), 16)
                        spl.inc(g.dma_start(out=oT_tlh[h],
                                            in_=OThh[:, OWN:OWN + 1]), 16)
                        spl.inc(g.dma_start(out=oT_tll[h],
                                            in_=OThl[:, OWN:OWN + 1]), 16)
                    prev["spl"] = spl.n

                heads_spl = spl.n

        with ExitStack() as p34:
            h1sb = p34.enter_context(
                nc.sbuf_tensor("h1sb", [128, CI * OWN], F32))
            bccol = p34.enter_context(nc.sbuf_tensor("bccol", [128, CI], F32))
            xnc = p34.enter_context(nc.sbuf_tensor("xnc", [128, CI], F32))

            with ExitStack() as ph:
                pe_ = ph.enter_context
                otbh = pe_(nc.sbuf_tensor("otbh", [128, NH * 512], BF16))
                otbl = pe_(nc.sbuf_tensor("otbl", [128, NH * 512], BF16))
                otbth = pe_(nc.sbuf_tensor("otbth", [128, NH], BF16))
                otbtl = pe_(nc.sbuf_tensor("otbtl", [128, NH], BF16))
                ottmp = pe_(nc.sbuf_tensor("ottmp", [16, 256], BF16))
                identb = pe_(nc.sbuf_tensor("identb", [16, 16], BF16))
                woutb = [pe_(nc.sbuf_tensor(f"woutb{i}", [128, 7 * 128], F32))
                         for i in range(2)]
                wpoh = [pe_(nc.sbuf_tensor(f"wpoh{i}", [128, 7 * 128], BF16))
                        for i in range(4)]
                wpol = [pe_(nc.sbuf_tensor(f"wpol{i}", [128, 7 * 128], BF16))
                        for i in range(4)]
                ytail = pe_(nc.sbuf_tensor("ytail", [128, CI], F32))
                x1sb = pe_(nc.sbuf_tensor("x1sb", [128, CI * OWN], F32))
                x0tm2 = [pe_(nc.sbuf_tensor(f"x0tm2{i}", [128, D], F32))
                         for i in range(2)]
                sqs2 = pe_(nc.sbuf_tensor("sqs2", [128, 1024], F32))
                x0col = pe_(nc.sbuf_tensor("x0col", [128, CI], F32))
                x1col = pe_(nc.sbuf_tensor("x1col", [128, CI], F32))
                tsc = pe_(nc.sbuf_tensor("tsc", [1, 8], F32))
                colsc = pe_(nc.sbuf_tensor("colsc", [128, 2], F32))
                psY = pe_(nc.psum_tensor("psY", [128, 3584], F32))
                psA = pe_(nc.psum_tensor("psA", [128, 512], F32))

                rounds = [[0, 1, 2, 3], [4, 5, 6], [7, 8, 9, 10],
                          [11, 12, 13], [14, 15]]
                roff = [0, 2048, 0, 2048, 0]
                seq = [(t2, r2, f2) for t2 in range(2) for r2 in range(5)
                       for f2 in range(NH)]
                wout_ld = {}
                wout_sp = {}
                wout_cons = {}

                def load_wout(q, key):
                    tt_, ri_, f_ = key
                    cis = rounds[ri_]
                    sb = woutb[f_ % 2]
                    i = seq.index(key)
                    if i >= 1:
                        dve.wait_at(q, wout_sp.get(seq[i - 1], 0))
                    src = wout_t[f_, cis[0]:cis[0] + len(cis)].transpose(
                        [1, 0, 2])
                    ldw[f_ % 2].inc(
                        q.dma_start(out=sb[:, 0:len(cis) * 128]
                                    .rearrange("p (a b) -> p a b",
                                               a=len(cis)),
                                    in_=src), 16)
                    wout_ld[key] = ldw[f_ % 2].n

                def split_wout(key):
                    tt_, ri_, f_ = key
                    nco = len(rounds[ri_]) * 128
                    i = seq.index(key)

                    @blk.scalar
                    def _(c, key=key, f_=f_, nco=nco, i=i):
                        c = _Ch(c, sc)
                        ldw[f_ % 2].wait_at(c, wout_ld[key])
                        if i >= 4:
                            pe.wait_at(c, wout_cons.get(seq[i - 4], 0))
                        sc.inc(c.copy(wpoh[i % 4][:, 0:nco],
                                      woutb[f_ % 2][:, 0:nco]))
                    cast_v = sc.n

                    @blk.vector
                    def _(v, key=key, i=i, f_=f_, nco=nco, cast_v=cast_v):
                        v = _Ch(v, dve)
                        sc.wait_at(v, cast_v)
                        dve.inc(v.tensor_sub(wpol[i % 4][:, 0:nco],
                                             woutb[f_ % 2][:, 0:nco],
                                             wpoh[i % 4][:, 0:nco]))
                    wout_sp[key] = dve.n

                @blk.sync
                def _(q):
                    spl.wait_at(q, heads_spl)
                    ldw[0].wait_at(q, ldw[0].n)
                    ldw[1].wait_at(q, ldw[1].n)
                    dve.wait(q)
                    sc.wait(q)
                    pe.wait(q)
                    gq.wait(q)
                    dbgs.wait(q)
                    for f in range(NH):
                        ldo.inc(q.dma_start(out=otbh[:, f * 512:(f + 1) * 512],
                                            in_=oT_sph[f][:, 0:512]), 16)
                        ldo.inc(q.dma_start(out=otbl[:, f * 512:(f + 1) * 512],
                                            in_=oT_spl[f][:, 0:512]), 16)
                    load_wout(q, seq[0])
                otb_ld0 = ldo.n

                @blk.sync
                def _(q):
                    ldm.inc(q.dma_start(
                        out=ottmp[:, 0:128],
                        in_=bass.AP(oT_tlh.tensor, 0, [[128, 16], [1, 128]])),
                        16)
                    ldm.inc(q.dma_start(
                        out=ottmp[:, 128:256],
                        in_=bass.AP(oT_tll.tensor, 0, [[128, 16], [1, 128]])),
                        16)
                tail_ld = ldm.n

                @blk.scalar
                def _(c):
                    c = _Ch(c, sc)
                    sc.inc(c.copy(identb[:], ident[0:16, 0:16]))
                idb_v = sc.n

                @blk.tensor
                def _(t):
                    ldm.wait_at(t, tail_ld)
                    sc.wait_at(t, idb_v)
                    t.matmul(psA[:, 32:40].bitcast(BF16), ottmp[:, 0:128],
                             identb[:], is_transpose=True, start=True,
                             stop=False)
                    pe.inc(t.matmul(psA[:, 40:48].bitcast(BF16),
                                    ottmp[:, 128:256],
                                    identb[:], is_transpose=True,
                                    start=False, stop=True))
                ttr_v = pe.n

                @blk.scalar
                def _(c):
                    c = _Ch(c, sc)
                    pe.wait_at(c, ttr_v)
                    c.copy(otbth[:], psA[:, 32:40].bitcast(BF16))
                    sc.inc(c.copy(otbtl[:], psA[:, 40:48].bitcast(BF16)))
                tail_ev = sc.n
                split_wout(seq[0])

                @blk.sync
                def _(q):
                    load_wout(q, seq[1])
                split_wout(seq[1])

                @blk.sync
                def _(q):
                    load_wout(q, seq[2])
                split_wout(seq[2])

                @blk.sync
                def _(q):
                    load_wout(q, seq[3])
                split_wout(seq[3])

                @blk.sync
                def _(q):
                    load_wout(q, seq[4])

                otb_ld = {0: otb_ld0}
                x0ld = {}
                x0tr = {}

                @blk.sync
                def _(g):
                    pe.wait_at(g, pe.n)
                    sc.wait_at(g, sc.n)
                    dve.wait_at(g, dve.n)
                    for k in range(2):
                        ldx0[k % 2].inc(g.dma_start(
                            out=x0tm2[k][:],
                            in_=x0l[k * 128:(k + 1) * 128, :]), 16)
                        x0ld[k] = ldx0[k % 2].n
                if debug:
                    @blk.gpsimd
                    def _(g):
                        sc.wait_at(g, tail_ev)
                        dbgs.inc(g.dma_start(out=dbg["dbg_otbt"],
                                             in_=otbth[:]), 16)
                ytail_ev = 0
                rnd_par = {0: 0, 1: 0}

                for tt in range(2):
                    for ri, cis in enumerate(rounds):
                        for f in range(NH):
                            key = (tt, ri, f)

                            @blk.tensor
                            def _(t, key=key, cis=cis, tt=tt, ri=ri, f=f):
                                i = seq.index(key)
                                off = roff[ri]
                                dve.wait_at(t, wout_sp[key])
                                ldo.wait_at(t, otb_ld[tt])
                                if f == 0:
                                    sc.wait_at(t, rnd_par[ri % 2])
                                oh = otbh[:, f * 512:(f + 1) * 512]
                                ol = otbl[:, f * 512:(f + 1) * 512]
                                for k8, ci in enumerate(cis):
                                    wh_ = wpoh[i % 4][:, k8 * 128:
                                                      (k8 + 1) * 128]
                                    wl_ = wpol[i % 4][:, k8 * 128:
                                                      (k8 + 1) * 128]
                                    reg = psY[:, off + k8 * 512:
                                              off + (k8 + 1) * 512]
                                    t.matmul(reg, wh_, oh,
                                             start=(f == 0), stop=False)
                                    t.matmul(reg, wh_, ol,
                                             start=False, stop=False)
                                    mm = t.matmul(reg, wl_, oh, start=False,
                                                  stop=(f == NH - 1))
                                if tt == 0:
                                    if ri == 0 and f == 0:
                                        sc.wait_at(t, tail_ev)
                                    for k8, ci in enumerate(cis):
                                        wh_ = wpoh[i % 4][:, k8 * 128:
                                                          (k8 + 1) * 128]
                                        wl_ = wpol[i % 4][:, k8 * 128:
                                                          (k8 + 1) * 128]
                                        reg = psA[:, ci:ci + 1]
                                        st_ = (ri == 0 and k8 == 0
                                               and f == 0)
                                        sp_ = (ri == 4 and k8 == 1
                                               and f == NH - 1)
                                        t.matmul(reg, wh_,
                                                 otbth[:, f:f + 1],
                                                 start=st_, stop=False)
                                        t.matmul(reg, wh_,
                                                 otbtl[:, f:f + 1],
                                                 start=False, stop=False)
                                        mm = t.matmul(reg, wl_,
                                                      otbth[:, f:f + 1],
                                                      start=False, stop=sp_)
                                pe.inc(mm)
                            wout_cons[key] = pe.n
                            i_ = seq.index(key)
                            if i_ + 4 < len(seq):
                                split_wout(seq[i_ + 4])

                            @blk.sync
                            def _(q, key=key):
                                i = seq.index(key)
                                if i + 5 < len(seq):
                                    load_wout(q, seq[i + 5])
                                if key == (0, 4, NH - 1):
                                    pe.wait_at(q, pe.n)
                                    for f2 in range(NH):
                                        ldo.inc(q.dma_start(
                                            out=otbh[:, f2 * 512:
                                                     (f2 + 1) * 512],
                                            in_=oT_sph[f2][:, 512:1024]), 16)
                                        ldo.inc(q.dma_start(
                                            out=otbl[:, f2 * 512:
                                                     (f2 + 1) * 512],
                                            in_=oT_spl[f2][:, 512:1024]), 16)
                                    otb_ld[1] = ldo.n
                        rnd_mm = pe.n

                        @blk.scalar
                        def _(c, cis=cis, rnd_mm=rnd_mm, tt=tt, ri=ri):
                            c = _Ch(c, sc)
                            pe.wait_at(c, rnd_mm)
                            off = roff[ri]
                            for k8, ci in enumerate(cis):
                                sc.inc(c.copy(
                                    x1sb[:, ci * OWN + tt * 512:
                                         ci * OWN + (tt + 1) * 512],
                                    psY[:, off + k8 * 512:
                                        off + (k8 + 1) * 512]))
                        if tt == 0 and ri == 4:
                            @blk.scalar
                            def _(c):
                                c = _Ch(c, sc)
                                sc.inc(c.copy(ytail[:], psA[:, 0:CI]))
                            ytail_ev = sc.n
                        rnd_par[ri % 2] = sc.n
                    y_ev = sc.n

                    for i in range(4):
                        k = tt * 4 + i
                        x0_v = x0ld[k]

                        for gg in range(4):
                            @blk.tensor
                            def _(t, tt=tt, i=i, k=k, gg=gg, x0_v=x0_v,
                                  y_ev=y_ev):
                                ldx0[k % 2].wait_at(t, x0_v)
                                sc.wait_at(t, y_ev)
                                dve.wait_at(t, dve.n)
                                for i4 in range(4):
                                    ci = gg * 4 + i4
                                    mm = t.matmul(
                                        psA[:, i4 * 128:(i4 + 1) * 128],
                                        x0tm2[k % 2][:, ci * 128:
                                                     (ci + 1) * 128],
                                        ident[:], is_transpose=True,
                                        start=(i4 == 0), stop=(i4 == 3))
                                pe.inc(mm)
                            trv = pe.n

                            @blk.vector
                            def _(v, tt=tt, i=i, gg=gg, trv=trv):
                                v = _Ch(v, dve)
                                pe.wait_at(v, trv)
                                t_off = tt * 512 + i * 128
                                o = x1sb[:].rearrange(
                                    "p (ci t) -> p ci t", ci=CI)[
                                    :, gg * 4:(gg + 1) * 4,
                                    t_off:t_off + 128]
                                dve.inc(v.tensor_tensor(
                                    o,
                                    psA[:].rearrange("p (a b) -> p a b", a=4),
                                    o, AOP.add))
                        x0tr[k] = pe.n

                        @blk.sync
                        def _(g, k=k):
                            if k + 2 < 8:
                                pe.wait_at(g, x0tr[k])
                                ldx0[k % 2].inc(g.dma_start(
                                    out=x0tm2[k % 2][:],
                                    in_=x0l[(k + 2) * 128:(k + 3) * 128, :]),
                                    16)
                                x0ld[k + 2] = ldx0[k % 2].n
                x1_dve = dve.n

                @blk.scalar
                def _(c):
                    c = _Ch(c, sc)
                    sc.wait_at(c, ytail_ev)
                    gq.inc(c.dma_start(
                        out=x0col[:],
                        in_=bass.AP(x0l.tensor, 2175 * D,
                                    [[1, 128], [128, CI]]),
                        allow_slow_non_contiguous=True), 16)
                x0c_v = gq.n

                @blk.vector
                def _(v, x0c_v=x0c_v):
                    v = _Ch(v, dve)
                    gq.wait_at(v, x0c_v)
                    dve.inc(v.tensor_add(x1col[:], x0col[:], ytail[:]))
                x1c_v = dve.n

                @blk.gpsimd
                def _(g):
                    dve.wait_at(g, x1_dve)
                    spl.inc(g.dma_start(out=x1_sp, in_=x1sb[:]), 16)
                    if debug:
                        dbgs.inc(g.dma_start(out=dbg["dbg_x1"],
                                              in_=x1sb[:]), 16)
                        dbgs.inc(g.dma_start(out=dbg["dbg_ytail"],
                                              in_=ytail[:]), 16)
                x1_spl = spl.n

                stat_mm = {}
                pe_base2 = pe.n
                for idx in range(2 * CI):
                    tt, ci = divmod(idx, CI)
                    so2 = (idx % 2) * 512

                    @blk.scalar
                    def _(c, tt=tt, ci=ci, so2=so2, idx=idx):
                        c = _Ch(c, sc)
                        dve.wait_at(c, x1_dve)
                        if idx == 0:
                            pe.wait_at(c, pe_base2)
                        else:
                            pe.wait_at(c, stat_mm.get(idx - 2, 0))
                        sc.inc(c.activation(
                            sqs2[:, so2:so2 + 512],
                            x1sb[:, ci * OWN + tt * 512:
                                 ci * OWN + (tt + 1) * 512], AFT.Square))
                    sqv = sc.n

                    @blk.tensor
                    def _(t, tt=tt, ci=ci, so2=so2, sqv=sqv, idx=idx):
                        if idx == 0:
                            dve.wait_at(t, x1_dve)
                        sc.wait_at(t, sqv)
                        t.matmul(psY[0:1, tt * 512:(tt + 1) * 512],
                                 onesf[:],
                                 x1sb[:, ci * OWN + tt * 512:
                                      ci * OWN + (tt + 1) * 512],
                                 start=(ci == 0), stop=(ci == CI - 1))
                        mm = t.matmul(
                            psY[0:1, 1024 + tt * 512:1536 + tt * 512],
                            onesf[:], sqs2[:, so2:so2 + 512],
                            start=(ci == 0), stop=(ci == CI - 1))
                        pe.inc(mm)
                    stat_mm[idx] = pe.n
                ln2s_v = pe.n

                @blk.vector
                def _(v):
                    v = _Ch(v, dve)
                    pe.wait_at(v, ln2s_v)
                    v.tensor_scalar_mul(sqs2[0:1, :], psY[0:1, 0:OWN],
                                        1.0 / D)
                    v.tensor_mul(x0tm2[0][0:1, 0:1024], sqs2[0:1, :],
                                 sqs2[0:1, :])
                    dve.inc(v.scalar_tensor_tensor(
                        x0tm2[0][0:1, 0:1024], psY[0:1, OWN:2 * OWN],
                        1.0 / D, x0tm2[0][0:1, 0:1024],
                        AOP.mult, AOP.subtract))
                var2_v = dve.n

                @blk.scalar
                def _(c):
                    c = _Ch(c, sc)
                    dve.wait_at(c, var2_v)
                    sc.inc(c.activation(x0tm2[0][0:1, 0:1024],
                                        x0tm2[0][0:1, 0:1024], AFT.Sqrt,
                                        bias=epsc[0:1, 0:1]))
                sd2_v = sc.n

                @blk.vector
                def _(v):
                    v = _Ch(v, dve)
                    sc.wait_at(v, sd2_v)
                    dve.inc(v.reciprocal(x0tm2[0][0:1, 0:1024],
                                         x0tm2[0][0:1, 0:1024]))
                r2_v = dve.n

                @blk.tensor
                def _(t):
                    dve.wait_at(t, r2_v)
                    for sl in range(2):
                        t.matmul(psY[:, sl * 512:(sl + 1) * 512], ones1r[:],
                                 sqs2[0:1, sl * 512:(sl + 1) * 512],
                                 start=True, stop=True)
                        mm = t.matmul(
                            psY[:, 1024 + sl * 512:1536 + sl * 512],
                            ones1r[:],
                            x0tm2[0][0:1, sl * 512:(sl + 1) * 512],
                            start=True, stop=True)
                    pe.inc(mm)
                rep2_v = pe.n

                for ci in range(CI):
                    @blk.vector
                    def _(v, ci=ci):
                        v = _Ch(v, dve)
                        pe.wait_at(v, rep2_v)
                        sl_ = x1sb[:, ci * OWN:(ci + 1) * OWN]
                        hs = h1sb[:, ci * OWN:(ci + 1) * OWN]
                        v.tensor_sub(hs, sl_, psY[:, 0:OWN])
                        dve.inc(v.tensor_mul(hs, hs, psY[:, 1024:2048]))
                h1_v = dve.n
                if debug:
                    @blk.gpsimd
                    def _(g):
                        dve.wait_at(g, h1_v)
                        dbgs.inc(g.dma_start(out=dbg["dbg_h1"],
                                              in_=h1sb[:]), 16)

                @blk.tensor
                def _(t):
                    dve.wait_at(t, x1c_v)
                    sc.wait_at(t, ytail_ev)
                    pe.inc(t.matmul(psA[0:1, 0:CI], onesf[:], x1col[:],
                                    start=True, stop=True))
                cs_v = pe.n

                @blk.scalar
                def _(c):
                    c = _Ch(c, sc)
                    dve.wait_at(c, x1c_v)
                    sc.inc(c.activation(xnc[:], x1col[:], AFT.Square))
                csq_v = sc.n

                @blk.tensor
                def _(t):
                    sc.wait_at(t, csq_v)
                    sc.wait_at(t, y_ev)
                    pe.inc(t.matmul(psY[0:1, 2048:2048 + CI], onesf[:],
                                    xnc[:], start=True, stop=True))
                csqs_v = pe.n

                @blk.vector
                def _(v):
                    v = _Ch(v, dve)
                    pe.wait_at(v, csqs_v)
                    v.reduce_sum(tsc[:, 0:1], psA[0:1, 0:CI], axis=AXX)
                    v.reduce_sum(tsc[:, 1:2], psY[0:1, 2048:2048 + CI],
                                 axis=AXX)
                    v.tensor_scalar_mul(tsc[:, 0:1], tsc[:, 0:1], 1.0 / D)
                    v.tensor_scalar_mul(tsc[:, 1:2], tsc[:, 1:2], 1.0 / D)
                    v.tensor_mul(tsc[:, 2:3], tsc[:, 0:1], tsc[:, 0:1])
                    dve.inc(v.tensor_sub(tsc[:, 2:3], tsc[:, 1:2],
                                         tsc[:, 2:3]))
                cvar_v = dve.n

                @blk.scalar
                def _(c):
                    c = _Ch(c, sc)
                    dve.wait_at(c, cvar_v)
                    sc.inc(c.activation(tsc[:, 3:4], tsc[:, 2:3], AFT.Sqrt,
                                        bias=epsc[0:1, 0:1]))
                csd_v = sc.n

                @blk.vector
                def _(v):
                    v = _Ch(v, dve)
                    sc.wait_at(v, csd_v)
                    v.reciprocal(tsc[:, 4:5], tsc[:, 3:4])
                    v.tensor_mul(tsc[:, 5:6], tsc[:, 4:5], tsc[:, 4:5])
                    dve.inc(v.tensor_mul(tsc[:, 5:6], tsc[:, 2:3],
                                         tsc[:, 5:6]))
                cvh_v = dve.n

                @blk.scalar
                def _(c):
                    c = _Ch(c, sc)
                    dve.wait_at(c, cvh_v)
                    sc.inc(c.activation(tsc[:, 6:7], tsc[:, 5:6], AFT.Sqrt,
                                        bias=epsc[0:1, 1:2]))
                cvsd_v = sc.n

                @blk.vector
                def _(v):
                    v = _Ch(v, dve)
                    sc.wait_at(v, cvsd_v)
                    v.reciprocal(tsc[:, 7:8], tsc[:, 6:7])
                    dve.inc(v.tensor_mul(tsc[:, 7:8], tsc[:, 4:5],
                                         tsc[:, 7:8]))
                s_v = dve.n

                @blk.tensor
                def _(t):
                    dve.wait_at(t, s_v)
                    t.matmul(psA[:, 64:65], ones1r[:], tsc[0:1, 0:1],
                             start=True, stop=False)
                    pe.inc(t.matmul(psA[:, 65:66], ones1r[:], tsc[0:1, 7:8],
                                    start=False, stop=True))
                srep_v = pe.n

                @blk.scalar
                def _(c):
                    c = _Ch(c, sc)
                    pe.wait_at(c, srep_v)
                    sc.inc(c.copy(colsc[:], psA[:, 64:66]))
                srep_ev = sc.n

                @blk.vector
                def _(v):
                    v = _Ch(v, dve)
                    sc.wait_at(v, srep_ev)
                    ldm.wait(v)
                    v.tensor_scalar(xnc[:], x1col[:], colsc[:, 0:1],
                                    colsc[:, 1:2], AOP.subtract, AOP.mult)
                    v.tensor_mul(xnc[:], xnc[:], colc["ing"][:])
                    dve.inc(v.tensor_add(xnc[:], xnc[:], colc["inb"][:]))
                xnc_v = dve.n

            with ExitStack() as ph:
                pe_ = ph.enter_context
                th4 = pe_(nc.sbuf_tensor("th4", [4, D], F32))
                tn4 = pe_(nc.sbuf_tensor("tn4", [4, D], F32))
                t14 = pe_(nc.sbuf_tensor("t14", [4, 1024], F32))
                tstat = pe_(nc.sbuf_tensor("tstat", [4, 8], F32))
                tcol4 = pe_(nc.sbuf_tensor("tcol4", [128, 64], F32))
                arb = pe_(nc.sbuf_tensor("arb", [4, D], F32))
                lwb = [pe_(nc.sbuf_tensor(f"lwb{i}", [128, 2048], F32))
                       for i in range(3)]
                rowv = {nm: pe_(nc.sbuf_tensor(nm + "_s", [4, w], F32))
                        for nm, w in (("png", D), ("pnb", D), ("aggb", D),
                                      ("bcb", D), ("d1b", 1024), ("d2b", D))}
                ohh = pe_(nc.sbuf_tensor("ohh_s", [1, 4], F32))
                oh1 = pe_(nc.sbuf_tensor("oh1_s", [4, 1], F32))
                psL = pe_(nc.psum_tensor("psL", [4, 2048], F32))
                psC = pe_(nc.psum_tensor("psC", [128, 64], F32))

                @blk.sync
                def _(q):
                    dve.wait_at(q, dve.n)
                    sc.wait(q)
                    pe.wait(q)
                    gq.wait(q)
                    spl.wait(q)
                    dbgs.wait(q)
                    for dst, src in [(rowv["png"], png_i), (rowv["pnb"], pnb_i),
                                     (rowv["aggb"], aggb_i),
                                     (rowv["bcb"], bcb_i),
                                     (rowv["d1b"], d1b_i),
                                     (rowv["d2b"], d2b_i),
                                     (ohh, ohh_i), (oh1, oh1_i)]:
                        ldm.inc(q.dma_start(out=dst[:], in_=src), 16)
                ldm4 = ldm.n

                lw_ld = {}
                lw_i = [0]
                ldw3 = [S["ldwa"], S["ldwb"], S["ldwc"]]
                lsrc = [agg_t[ci].transpose([1, 0, 2]) for ci in range(CI)]
                for it in range(4):
                    lsrc += [d1s_t[z].transpose([1, 0, 2]) for z in range(8)]
                    lsrc += [d2s_t[z].transpose([1, 0, 2]) for z in range(8)]
                lsrc += [bc_t[ci].transpose([1, 0, 2]) for ci in range(CI)]

                def stream_lw(q):
                    i = lw_i[0]
                    ldw3[i % 3].inc(
                        q.dma_start(out=lwb[i % 3][:]
                                    .rearrange("p (a b) -> p a b", a=4),
                                    in_=lsrc[i]), 16)
                    lw_ld[i] = (i % 3, ldw3[i % 3].n)
                    lw_i[0] += 1

                def lw_next(q):
                    if lw_i[0] < len(lsrc):
                        pe.wait_at(q, pe.n)
                        stream_lw(q)

                @blk.sync
                def _(q):
                    dve.wait_at(q, xnc_v)
                    ldw[0].wait_at(q, ldw[0].n)
                    ldw[1].wait_at(q, ldw[1].n)
                    stream_lw(q)
                    stream_lw(q)
                    stream_lw(q)

                @blk.tensor
                def _(t):
                    dve.wait_at(t, xnc_v)
                    sc.wait_at(t, sc.n)
                    ldm.wait_at(t, ldm4)
                    for i in range(CI):
                        mm = t.matmul(psL[0:1, i * 128:(i + 1) * 128],
                                      xnc[:, i:i + 1], ident[:],
                                      is_transpose=True,
                                      start=(i % 4 == 0),
                                      stop=(i % 4 == 3))
                    pe.inc(mm)
                xr_v = pe.n

                @blk.scalar
                def _(c):
                    c = _Ch(c, sc)
                    pe.wait_at(c, xr_v)
                    sc.inc(c.copy(arb[0:1, :], psL[0:1, :]))
                xr_ev = sc.n

                @blk.tensor
                def _(t):
                    sc.wait_at(t, xr_ev)
                    for sl in range(4):
                        mm = t.matmul(psL[:, sl * 512:(sl + 1) * 512],
                                      ohh[:], arb[0:1, sl * 512:(sl + 1) * 512],
                                      start=True, stop=True)
                    pe.inc(mm)
                ai_v = pe.n

                @blk.scalar
                def _(c):
                    c = _Ch(c, sc)
                    pe.wait_at(c, ai_v)
                    sc.inc(c.copy(arb[:], psL[:]))
                ai_ev = sc.n

                CC = S["cc"]
                CCD = S["ccd"]

                def do_allreduce(wait_sc_v):
                    @blk.gpsimd
                    def _(g, wait_sc_v=wait_sc_v):
                        sc.wait_at(g, wait_sc_v)
                        CCD.inc(g.dma_start(out=arin, in_=arb[:]), 16)
                        g.wait_ge(CCD.h, CCD.n)
                        CC.inc(g.collective_compute(
                            "AllReduce", AOP.add,
                            replica_groups=[list(range(8))],
                            ins=[arin.opt()], outs=[arout.opt()]))
                        g.wait_ge(CC.h, CC.n)
                        CCD.inc(g.dma_start(out=arb[:], in_=arout), 16)
                    return CCD.n

                ar_v = do_allreduce(ai_ev)

                def vec2col4(src, n, wait_pe_list, extra=0):
                    @blk.tensor
                    def _(t):
                        for w in wait_pe_list:
                            w(t)
                        for i in range(n):
                            mm = t.matmul(psC[:, i * 4:(i + 1) * 4],
                                          src[0:4, i * 128:(i + 1) * 128],
                                          ident[0:4, 0:4], is_transpose=True,
                                          start=(i == 0), stop=(i == n - 1))
                        pe.inc(mm)
                    vv = pe.n

                    @blk.scalar
                    def _(c, vv=vv, n=n):
                        c = _Ch(c, sc)
                        pe.wait_at(c, vv)
                        sc.inc(c.copy(tcol4[:, 0:n * 4], psC[:, 0:n * 4]))
                    return sc.n

                tc_v = vec2col4(arb, CI,
                                [lambda t: CCD.wait_at(t, ar_v),
                                 lambda t: sc.wait_at(t, sc.n)])

                for ci in range(CI):
                    @blk.tensor
                    def _(t, ci=ci):
                        bi, v = lw_ld[ci]
                        ldw3[bi].wait_at(t, v)
                        sc.wait_at(t, tc_v)
                        for ft in range(4):
                            mm = t.matmul(
                                psL[:, ft * 512:(ft + 1) * 512],
                                tcol4[:, ci * 4:(ci + 1) * 4],
                                lwb[ci % 3][:, ft * 512:(ft + 1) * 512],
                                start=(ci == 0), stop=(ci == CI - 1))
                        pe.inc(mm)

                    @blk.sync
                    def _(q):
                        lw_next(q)
                agg_v = pe.n

                @blk.vector
                def _(v):
                    v = _Ch(v, dve)
                    pe.wait_at(v, agg_v)
                    ldm.wait_at(v, ldm4)
                    dve.inc(v.tensor_add(th4[:], psL[:], rowv["aggb"][:]))
                th_v = dve.n

                for it in range(4):
                    @blk.vector
                    def _(v):
                        v = _Ch(v, dve)
                        dve.wait_at(v, dve.n)
                        dve.inc(v.reduce_sum(tstat[:, 0:1], th4[:], axis=AXX))
                    ts_v = dve.n

                    @blk.scalar
                    def _(c, ts_v=ts_v):
                        c = _Ch(c, sc)
                        dve.wait_at(c, ts_v)
                        sc.inc(c.activation(tn4[:], th4[:], AFT.Square,
                                            accum_out=tstat[:, 1:2]))
                    tsq_v = sc.n

                    @blk.vector
                    def _(v, tsq_v=tsq_v):
                        v = _Ch(v, dve)
                        sc.wait_at(v, tsq_v)
                        v.tensor_scalar_mul(tstat[:, 0:1], tstat[:, 0:1],
                                            1.0 / D)
                        v.tensor_scalar_mul(tstat[:, 1:2], tstat[:, 1:2],
                                            1.0 / D)
                        v.tensor_mul(tstat[:, 2:3], tstat[:, 0:1],
                                     tstat[:, 0:1])
                        dve.inc(v.tensor_sub(tstat[:, 2:3], tstat[:, 1:2],
                                             tstat[:, 2:3]))
                    tv_v = dve.n

                    @blk.scalar
                    def _(c, tv_v=tv_v):
                        c = _Ch(c, sc)
                        dve.wait_at(c, tv_v)
                        sc.inc(c.activation(tstat[:, 3:4], tstat[:, 2:3],
                                            AFT.Sqrt, bias=epsc[0:4, 1:2]))
                    tsd_v = sc.n

                    @blk.vector
                    def _(v, tsd_v=tsd_v):
                        v = _Ch(v, dve)
                        sc.wait_at(v, tsd_v)
                        v.reciprocal(tstat[:, 4:5], tstat[:, 3:4])
                        v.tensor_scalar(tn4[:], th4[:], tstat[:, 0:1],
                                        tstat[:, 4:5], AOP.subtract, AOP.mult)
                        v.tensor_mul(tn4[:], tn4[:], rowv["png"][:])
                        dve.inc(v.tensor_add(tn4[:], tn4[:], rowv["pnb"][:]))
                    tn_v = dve.n

                    tcv = vec2col4(tn4, CI,
                                   [lambda t, tn_v=tn_v: dve.wait_at(t, tn_v),
                                    lambda t: sc.wait_at(t, sc.n)])

                    for z in range(8):
                        ht, cg = divmod(z, 4)
                        li = CI + it * 16 + z

                        @blk.tensor
                        def _(t, z=z, ht=ht, cg=cg, li=li, tcv=tcv):
                            bi, v = lw_ld[li]
                            ldw3[bi].wait_at(t, v)
                            sc.wait_at(t, tcv)
                            if z == 0:
                                dve.wait_at(t, dve.n)
                            for c4 in range(4):
                                ci = cg * 4 + c4
                                mm = t.matmul(
                                    psL[:, ht * 512:(ht + 1) * 512],
                                    tcol4[:, ci * 4:(ci + 1) * 4],
                                    lwb[li % 3][:, c4 * 512:(c4 + 1) * 512],
                                    start=(ci == 0), stop=(ci == CI - 1))
                            pe.inc(mm)

                        @blk.sync
                        def _(q):
                            lw_next(q)
                    d1_v = pe.n

                    @blk.scalar
                    def _(c, d1_v=d1_v):
                        c = _Ch(c, sc)
                        pe.wait_at(c, d1_v)
                        sc.inc(c.copy(t14[:], psL[:, 0:1024]))
                    d1_ev = sc.n

                    @blk.vector
                    def _(v, d1_ev=d1_ev):
                        v = _Ch(v, dve)
                        sc.wait_at(v, d1_ev)
                        dve.inc(v.tensor_add(t14[:], t14[:], rowv["d1b"][:]))
                    d1b_v = dve.n

                    @blk.scalar
                    def _(c, d1b_v=d1b_v):
                        c = _Ch(c, sc)
                        dve.wait_at(c, d1b_v)
                        sc.inc(c.activation(t14[:], t14[:], AFT.Gelu))
                    gel_v = sc.n

                    t1cv = vec2col4(t14, 8,
                                    [lambda t, gel_v=gel_v: sc.wait_at(t, gel_v)])

                    for hc in range(8):
                        li = CI + it * 16 + 8 + hc

                        @blk.tensor
                        def _(t, hc=hc, li=li, t1cv=t1cv):
                            bi, v = lw_ld[li]
                            ldw3[bi].wait_at(t, v)
                            sc.wait_at(t, t1cv)
                            if hc == 0:
                                dve.wait_at(t, dve.n)
                            for ft in range(4):
                                mm = t.matmul(
                                    psL[:, ft * 512:(ft + 1) * 512],
                                    tcol4[:, hc * 4:(hc + 1) * 4],
                                    lwb[li % 3][:, ft * 512:(ft + 1) * 512],
                                    start=(hc == 0), stop=(hc == 7))
                            pe.inc(mm)

                        @blk.sync
                        def _(q):
                            lw_next(q)
                    d2_v = pe.n

                    @blk.scalar
                    def _(c, d2_v=d2_v):
                        c = _Ch(c, sc)
                        pe.wait_at(c, d2_v)
                        sc.inc(c.copy(arb[:], psL[:]))
                    d2_ev = sc.n

                    ar2_v = do_allreduce(d2_ev)

                    @blk.vector
                    def _(v, ar2_v=ar2_v):
                        v = _Ch(v, dve)
                        CCD.wait_at(v, ar2_v)
                        v.tensor_add(th4[:], th4[:], arb[:])
                        dve.inc(v.tensor_add(th4[:], th4[:], rowv["d2b"][:]))

                thc_v = vec2col4(th4, CI,
                                 [lambda t: dve.wait_at(t, dve.n),
                                  lambda t: sc.wait_at(t, sc.n)])

                for ci in range(CI):
                    li = CI + 64 + ci

                    @blk.tensor
                    def _(t, ci=ci, li=li):
                        bi, v = lw_ld[li]
                        ldw3[bi].wait_at(t, v)
                        sc.wait_at(t, thc_v)
                        if ci == 0:
                            dve.wait_at(t, dve.n)
                        for ft in range(4):
                            mm = t.matmul(
                                psL[:, ft * 512:(ft + 1) * 512],
                                tcol4[:, ci * 4:(ci + 1) * 4],
                                lwb[li % 3][:, ft * 512:(ft + 1) * 512],
                                start=(ci == 0), stop=(ci == CI - 1))
                        pe.inc(mm)

                    @blk.sync
                    def _(q):
                        lw_next(q)
                bc_v = pe.n

                @blk.vector
                def _(v):
                    v = _Ch(v, dve)
                    pe.wait_at(v, bc_v)
                    dve.inc(v.tensor_add(arb[:], psL[:], rowv["bcb"][:]))
                bcv_v = dve.n

                @blk.tensor
                def _(t):
                    dve.wait_at(t, bcv_v)
                    for sl in range(4):
                        mm = t.matmul(psL[0:1, sl * 512:(sl + 1) * 512],
                                      oh1[:], arb[:, sl * 512:(sl + 1) * 512],
                                      start=True, stop=True)
                    pe.inc(mm)
                bo_v = pe.n

                @blk.scalar
                def _(c):
                    c = _Ch(c, sc)
                    pe.wait_at(c, bo_v)
                    sc.inc(c.copy(tn4[0:1, :], psL[0:1, :]))
                bo_ev = sc.n

                @blk.tensor
                def _(t):
                    sc.wait_at(t, bo_ev)
                    for i in range(CI):
                        mm = t.matmul(psC[:, i:i + 1],
                                      tn4[0:1, i * 128:(i + 1) * 128],
                                      ident[0:1, 0:1], is_transpose=True,
                                      start=(i == 0), stop=(i == 15))
                    pe.inc(mm)
                bcc_v = pe.n

                @blk.scalar
                def _(c):
                    c = _Ch(c, sc)
                    pe.wait_at(c, bcc_v)
                    sc.inc(c.copy(bccol[:], psC[:, 0:16]))
                bccol_v = sc.n

                if debug:
                    @blk.gpsimd
                    def _(g):
                        sc.wait_at(g, bccol_v)
                        dbgs.inc(g.dma_start(out=dbg["dbg_th"],
                                             in_=th4[0:1, :]), 16)
                        dbgs.inc(g.dma_start(out=dbg["dbg_bc"],
                                             in_=bccol[:]), 16)

            with ExitStack() as ph:
                pe_ = ph.enter_context
                gwb = [pe_(nc.sbuf_tensor(f"gwb{i}", [128, CI * 128], F32))
                       for i in range(2)]
                gatesb = pe_(nc.sbuf_tensor("gatesb", [128, CI * OWN], F32))
                sqs3 = pe_(nc.sbuf_tensor("sqs3", [128, 1024], F32))
                st3 = pe_(nc.sbuf_tensor("st3", [1, 3 * OWN], F32))
                musc3 = pe_(nc.sbuf_tensor("musc3", [128, OWN], F32))
                rsc3 = pe_(nc.sbuf_tensor("rsc3", [128, OWN], F32))
                x1rb = [pe_(nc.sbuf_tensor(f"x1rb{i}", [128, OWN], F32))
                        for i in range(4)]
                outtm = pe_(nc.sbuf_tensor("outtm", [128, D], F32))
                psV1b = pe_(nc.psum_tensor("psV1b", [128, 2048], F32))
                psG = [pe_(nc.psum_tensor(f"psG{i}", [128, 512], F32))
                       for i in range(2)]

                g_ld = {}

                def load_g(q, co):
                    src = gate_t[co].transpose([1, 0, 2])
                    ldw[co % 2].inc(
                        q.dma_start(out=gwb[co % 2][:]
                                    .rearrange("p (a b) -> p a b", a=CI),
                                    in_=src), 16)
                    g_ld[co] = ldw[co % 2].n

                @blk.sync
                def _(q):
                    sc.wait(q)
                    dve.wait(q)
                    pe.wait(q)
                    dbgs.wait(q)
                    ldw[0].wait_at(q, ldw[0].n)
                    ldw[1].wait_at(q, ldw[1].n)
                    load_g(q, 0)
                    load_g(q, 1)

                gev = {}
                for co in range(CI):
                    @blk.sync
                    def _(q, co=co):
                        if co >= 2:
                            pe.wait_at(q, pe.n)
                            load_g(q, co)
                    for tt in range(2):
                        pg = (co * 2 + tt) % 2

                        @blk.tensor
                        def _(t, co=co, tt=tt, pg=pg):
                            ldw[co % 2].wait_at(t, g_ld[co])
                            dve.wait_at(t, h1_v)
                            sc.wait_at(t, gev.get((co * 2 + tt) - 2, 0))
                            for ci in range(CI):
                                mm = t.matmul(
                                    psG[pg][:],
                                    gwb[co % 2][:, ci * 128:(ci + 1) * 128],
                                    h1sb[:, ci * OWN + tt * 512:
                                         ci * OWN + (tt + 1) * 512],
                                    start=(ci == 0), stop=(ci == CI - 1))
                            pe.inc(mm)
                        gmm = pe.n

                        @blk.scalar
                        def _(c, co=co, tt=tt, pg=pg, gmm=gmm):
                            c = _Ch(c, sc)
                            pe.wait_at(c, gmm)
                            sc.inc(c.activation(
                                gatesb[:, co * OWN + tt * 512:
                                       co * OWN + (tt + 1) * 512],
                                psG[pg][:], AFT.Sigmoid,
                                bias=colc["gb"][:, co:co + 1]))
                        gev[co * 2 + tt] = sc.n
                gate_v = sc.n
                x1r_ld = {}

                @blk.gpsimd
                def _(g):
                    spl.wait_at(g, x1_spl)
                    dve.wait_at(g, dve.n)
                    sc.wait_at(g, sc.n)
                    pe.wait_at(g, pe.n)
                    for ci in (0, 2):
                        ldx1[0].inc(g.dma_start(
                            out=x1rb[ci][:],
                            in_=x1_sp[:, ci * OWN:(ci + 1) * OWN]), 16)
                        x1r_ld[ci] = ldx1[0].n

                @blk.sync
                def _(q):
                    spl.wait_at(q, x1_spl)
                    dve.wait_at(q, dve.n)
                    sc.wait_at(q, sc.n)
                    pe.wait_at(q, pe.n)
                    for ci in (1, 3):
                        ldx1[1].inc(q.dma_start(
                            out=x1rb[ci][:],
                            in_=x1_sp[:, ci * OWN:(ci + 1) * OWN]), 16)
                        x1r_ld[ci] = ldx1[1].n
                dbgg_v = dbgs.n
                if debug:
                    @blk.gpsimd
                    def _(g):
                        sc.wait_at(g, gate_v)
                        dbgs.inc(g.dma_start(out=dbg["dbg_gate"],
                                              in_=gatesb[:]), 16)
                    dbgg_v = dbgs.n

                for ci in range(CI):
                    @blk.vector
                    def _(v, ci=ci):
                        v = _Ch(v, dve)
                        sc.wait_at(v, gate_v)
                        sc.wait_at(v, bccol_v)
                        dbgs.wait_at(v, dbgg_v)
                        sl_ = gatesb[:, ci * OWN:(ci + 1) * OWN]
                        dve.inc(v.scalar_tensor_tensor(
                            sl_, sl_, bccol[:, ci:ci + 1],
                            h1sb[:, ci * OWN:(ci + 1) * OWN],
                            AOP.mult, AOP.add))
                of_v = dve.n

                stat_mm3 = {}
                pe_base3 = pe.n
                for idx in range(2 * CI):
                    tt, ci = divmod(idx, CI)
                    so3 = (idx % 2) * 512

                    @blk.scalar
                    def _(c, tt=tt, ci=ci, so3=so3, idx=idx):
                        c = _Ch(c, sc)
                        dve.wait_at(c, of_v)
                        if idx == 0:
                            pe.wait_at(c, pe_base3)
                        else:
                            pe.wait_at(c, stat_mm3.get(idx - 2, 0))
                        sc.inc(c.activation(
                            sqs3[:, so3:so3 + 512],
                            gatesb[:, ci * OWN + tt * 512:
                                   ci * OWN + (tt + 1) * 512],
                            AFT.Square))
                    sq3v = sc.n

                    @blk.tensor
                    def _(t, tt=tt, ci=ci, so3=so3, sq3v=sq3v, idx=idx):
                        if idx == 0:
                            dve.wait_at(t, of_v)
                        sc.wait_at(t, sq3v)
                        t.matmul(psV1b[0:1, tt * 512:(tt + 1) * 512],
                                 onesf[:],
                                 gatesb[:, ci * OWN + tt * 512:
                                        ci * OWN + (tt + 1) * 512],
                                 start=(ci == 0), stop=(ci == CI - 1))
                        mm = t.matmul(
                            psV1b[0:1, 1024 + tt * 512:1536 + tt * 512],
                            onesf[:], sqs3[:, so3:so3 + 512],
                            start=(ci == 0), stop=(ci == CI - 1))
                        pe.inc(mm)
                    stat_mm3[idx] = pe.n
                ol_v = pe.n

                @blk.vector
                def _(v):
                    v = _Ch(v, dve)
                    pe.wait_at(v, ol_v)
                    v.tensor_scalar_mul(st3[0:1, 0:OWN], psV1b[0:1, 0:OWN],
                                        1.0 / D)
                    v.tensor_scalar_mul(st3[0:1, 2 * OWN:3 * OWN],
                                        psV1b[0:1, OWN:2 * OWN], 1.0 / D)
                    v.tensor_mul(st3[0:1, OWN:2 * OWN], st3[0:1, 0:OWN],
                                 st3[0:1, 0:OWN])
                    dve.inc(v.tensor_sub(st3[0:1, OWN:2 * OWN],
                                         st3[0:1, 2 * OWN:3 * OWN],
                                         st3[0:1, OWN:2 * OWN]))
                var3_v = dve.n

                @blk.scalar
                def _(c):
                    c = _Ch(c, sc)
                    dve.wait_at(c, var3_v)
                    sc.inc(c.activation(st3[0:1, OWN:2 * OWN],
                                        st3[0:1, OWN:2 * OWN], AFT.Sqrt,
                                        bias=epsc[0:1, 1:2]))
                sd3_v = sc.n

                @blk.vector
                def _(v):
                    v = _Ch(v, dve)
                    sc.wait_at(v, sd3_v)
                    dve.inc(v.reciprocal(st3[0:1, OWN:2 * OWN],
                                         st3[0:1, OWN:2 * OWN]))
                r3_v = dve.n

                @blk.tensor
                def _(t):
                    dve.wait_at(t, r3_v)
                    for sl in range(2):
                        t.matmul(psV1b[:, sl * 512:(sl + 1) * 512], ones1r[:],
                                 st3[0:1, sl * 512:(sl + 1) * 512],
                                 start=True, stop=True)
                        mm = t.matmul(
                            psV1b[:, 1024 + sl * 512:1536 + sl * 512],
                            ones1r[:],
                            st3[0:1, OWN + sl * 512:OWN + (sl + 1) * 512],
                            start=True, stop=True)
                    pe.inc(mm)
                rep3_v = pe.n

                @blk.scalar
                def _(c):
                    c = _Ch(c, sc)
                    pe.wait_at(c, rep3_v)
                    c.copy(musc3[:], psV1b[:, 0:OWN])
                    sc.inc(c.copy(rsc3[:], psV1b[:, 1024:2048]))
                rep3_ev = sc.n

                x2sb = h1sb
                for ci in range(CI):
                    @blk.vector
                    def _(v, ci=ci):
                        v = _Ch(v, dve)
                        sc.wait_at(v, rep3_ev)
                        sl_ = gatesb[:, ci * OWN:(ci + 1) * OWN]
                        v.tensor_sub(sl_, sl_, musc3[:])
                        dve.inc(v.tensor_mul(sl_, sl_, rsc3[:]))
                    n3v = dve.n

                    @blk.vector
                    def _(v, ci=ci, n3v=n3v):
                        v = _Ch(v, dve)
                        dve.wait_at(v, n3v)
                        v.tensor_scalar(
                            x2sb[:, ci * OWN:(ci + 1) * OWN],
                            gatesb[:, ci * OWN:(ci + 1) * OWN],
                            colc["og"][:, ci:ci + 1], colc["ob"][:, ci:ci + 1],
                            AOP.mult, AOP.add)
                        ldx1[ci % 2].wait(v)
                        sl_ = x2sb[:, ci * OWN:(ci + 1) * OWN]
                        dve.inc(v.tensor_add(sl_, sl_, x1rb[ci % 4][:]))
                    addv = dve.n

                    @blk.gpsimd
                    def _(g, ci=ci, addv=addv):
                        if ci + 4 < CI and ci % 2 == 0:
                            dve.wait_at(g, addv)
                            ldx1[0].inc(g.dma_start(
                                out=x1rb[ci % 4][:],
                                in_=x1_sp[:, (ci + 4) * OWN:(ci + 5) * OWN]),
                                16)
                            x1r_ld[ci + 4] = ldx1[0].n

                    @blk.sync
                    def _(q, ci=ci, addv=addv):
                        if ci + 4 < CI and ci % 2 == 1:
                            dve.wait_at(q, addv)
                            ldx1[1].inc(q.dma_start(
                                out=x1rb[ci % 4][:],
                                in_=x1_sp[:, (ci + 4) * OWN:(ci + 5) * OWN]),
                                16)
                            x1r_ld[ci + 4] = ldx1[1].n
                x2_v = dve.n

                out_dma = 0
                for tc in range(8):
                    for gg in range(4):
                        @blk.tensor
                        def _(t, tc=tc, gg=gg, out_dma=out_dma):
                            dve.wait_at(t, x2_v)
                            sc.wait_at(t, sc.n)
                            if gg == 0:
                                out_s.wait_at(t, out_dma)
                            for i4 in range(4):
                                ci = gg * 4 + i4
                                mm = t.matmul(
                                    psG[0][:, i4 * 128:(i4 + 1) * 128],
                                    x2sb[:, ci * OWN + tc * 128:
                                         ci * OWN + (tc + 1) * 128],
                                    ident[:], is_transpose=True,
                                    start=(i4 == 0), stop=(i4 == 3))
                            pe.inc(mm)
                        ftr = pe.n

                        @blk.scalar
                        def _(c, tc=tc, gg=gg, ftr=ftr):
                            c = _Ch(c, sc)
                            pe.wait_at(c, ftr)
                            dst = outtm[:].rearrange(
                                "p (ci t) -> p ci t", ci=CI)[
                                :, gg * 4:(gg + 1) * 4, 0:128]
                            sc.inc(c.copy(dst, psG[0][:]
                                          .rearrange("p (a b) -> p a b",
                                                     a=4)))

                    @blk.scalar
                    def _(c, tc=tc):
                        c = _Ch(c, sc)
                        sc.wait_at(c, sc.n)
                        out_s.inc(c.dma_start(
                            out=y_out[tc * 128:(tc + 1) * 128, :],
                            in_=outtm[:]), 16)
                    out_dma = out_s.n

                @blk.scalar
                def _(c):
                    c = _Ch(c, sc)
                    out_s.wait_at(c, out_s.n)
                    gq.wait_at(c, gq.n)
                    spl.wait_at(c, spl.n)
                    dbgs.wait_at(c, dbgs.n)
                    S["ccd"].wait_at(c, S["ccd"].n)
    return nc


_PROG_CACHE = {}


def _get_prog(debug=False):
    key = bool(debug)
    if key not in _PROG_CACHE:
        _PROG_CACHE[key] = _build(key)
    return _PROG_CACHE[key]


def _host_prep(inputs):
    x = np.ascontiguousarray(np.asarray(inputs["L0_attn_x"], dtype=np.float32))
    wqkv = np.asarray(inputs["L0_attn_w_qkv"], dtype=np.float32)
    wout = np.asarray(inputs["L0_attn_w_out"], dtype=np.float32)
    gw = np.asarray(inputs["L0_ffn_gate_w"], dtype=np.float32)
    aggw = np.asarray(inputs["L0_ffn_agg_w"], dtype=np.float32)
    bcw = np.asarray(inputs["L0_ffn_bc_w"], dtype=np.float32)
    d1w = np.asarray(inputs["L0_ffn_d1_w"], dtype=np.float32)
    d2w = np.asarray(inputs["L0_ffn_d2_w"], dtype=np.float32)

    wq_t = np.ascontiguousarray(
        wqkv.T.reshape(CI, 128, 3, NH, 128).transpose(3, 2, 0, 1, 4)
        .reshape(3 * NH, CI, 128, 128))
    W = np.ascontiguousarray(wout.T)
    wout_t = np.ascontiguousarray(
        W.reshape(NH, 128, CI, 128).transpose(0, 2, 1, 3))
    gate_t = np.ascontiguousarray(
        gw.reshape(CI, 128, CI, 128).transpose(2, 0, 1, 3))
    agg_t = np.ascontiguousarray(
        aggw.reshape(CI, 128, 4, 512).transpose(0, 2, 1, 3))
    bc_t = np.ascontiguousarray(
        bcw.reshape(CI, 128, 4, 512).transpose(0, 2, 1, 3))


    def colform(v):
        return np.ascontiguousarray(
            np.asarray(v, np.float32).reshape(CI, 128).T)

    def rowform(v):
        return np.ascontiguousarray(np.asarray(v, np.float32)[None, :])

    def rowform4(v):
        return np.ascontiguousarray(
            np.broadcast_to(np.asarray(v, np.float32)[None, :], (4, len(v)))
            .copy())

    common = {
        "wq_t": wq_t, "wout_t": wout_t, "gate_t": gate_t,
        "agg_t": agg_t, "bc_t": bc_t,
        "ident": np.eye(128, dtype=np.float32),
        "ing_c": colform(inputs["L0_ffn_in_gamma"]),
        "inb_c": colform(inputs["L0_ffn_in_beta"]),
        "og_c": colform(inputs["L0_ffn_out_gamma"]),
        "ob_c": colform(inputs["L0_ffn_out_beta"]),
        "gb_c": colform(inputs["L0_ffn_gate_b"]),
        "png_r": rowform4(inputs["L0_ffn_pn_gamma"]),
        "pnb_r": rowform4(inputs["L0_ffn_pn_beta"]),
        "aggb_r": rowform4(inputs["L0_ffn_agg_b"]),
        "bcb_r": rowform4(inputs["L0_ffn_bc_b"]),
        "d2b_r": rowform4(inputs["L0_ffn_d2_b"]),
    }

    inv = (10000.0 ** (-(np.arange(16) / 16.0))).astype(np.float64)
    maskD = np.zeros((128, 128), np.float32)
    for kk in range(128):
        maskD[kk, :kk] = NEGBIG

    in_maps, perms = [], []
    for c in range(8):
        b, par = divmod(c, 2)
        order = list(range(par, 16, 2)) + list(range(1 - par, 16, 2))
        perm = np.concatenate([np.arange(g * 128, (g + 1) * 128)
                               for g in order])
        perms.append(perm)
        x0l = np.concatenate([x[b][perm], x[b][1920:2048]], axis=0)
        posK = perm.astype(np.float64)
        angK = posK[None, :] * inv[:, None]
        posQ = np.concatenate([perm[:OWN].astype(np.float64), [2047.0]])
        angQ = posQ[None, :] * inv[:, None]
        angA = np.concatenate([angK, 2047.0 * inv[:, None]], axis=1)
        cosTt = np.zeros((32, T + 1), np.float32)
        sinTt = np.zeros((32, T + 1), np.float32)
        cosTt[0:16] = cosTt[16:32] = np.cos(angA)
        sinTt[0:16] = -np.sin(angA)
        sinTt[16:32] = np.sin(angA)
        maskO = (np.full((128, 128), NEGBIG, np.float32) if par == 0
                 else np.zeros((128, 128), np.float32))
        hs = slice(c * 1024, (c + 1) * 1024)
        d1s = np.ascontiguousarray(
            d1w[:, hs].reshape(CI, 128, 2, 512).transpose(2, 0, 1, 3)
            .reshape(2, 4, 4, 128, 512).reshape(8, 4, 128, 512))
        d2s = np.ascontiguousarray(
            d2w[hs, :].reshape(8, 128, 4, 512).transpose(0, 2, 1, 3))
        d1bs = np.ascontiguousarray(
            np.broadcast_to(np.asarray(inputs["L0_ffn_d1_b"], np.float32)
                            [None, hs], (4, 1024)).copy())
        ohh = np.zeros((1, 4), np.float32)
        ohh[0, b] = 0.5
        oh1 = np.zeros((4, 1), np.float32)
        oh1[b, 0] = 1.0
        m = dict(common)
        m.update({"x0l": np.ascontiguousarray(x0l), "cosT": cosTt,
                  "sinT": sinTt, "maskD": maskD, "maskO": maskO,
                  "d1s_t": d1s, "d2s_t": d2s, "d1b_r": d1bs,
                  "ohh_r": ohh, "oh1_r": oh1})
        in_maps.append(m)
    return in_maps, perms


def kernel(**inputs):
    in_maps, perms = _host_prep(inputs)
    nc = _get_prog(debug=False)
    res = run_bass_kernel_spmd(nc, in_maps, core_ids=list(range(8)))
    out = np.empty((4, T, D), dtype=np.float32)
    for c in range(8):
        out[c // 2][perms[c][:OWN]] = res.results[c]["y_out"]
    return out


# version 17
# speedup vs baseline: 1.0769x; 1.0769x over previous
import math
import numpy as np
from contextlib import ExitStack

import concourse.bass as bass
import concourse.mybir as mybir
from concourse.bass_utils import run_bass_kernel_spmd

F32 = mybir.dt.float32
BF16 = mybir.dt.bfloat16
AFT = mybir.ActivationFunctionType
AOP = mybir.AluOpType
AXX = mybir.AxisListType.X

D = 2048
CI = 16
T = 2048
TCH = 17
TLOC = TCH * 128
XCOL = 2049
NH = 16
OWN = 1024
NEGBIG = -1.0e30
QSCALE = 1.0 / math.sqrt(128.0)
EPS1 = 1e-5
EPS2 = 1e-6


class _Sem:
    def __init__(self, stack, nc, name):
        self.h = stack.enter_context(nc.semaphore(name))
        self.n = 0

    def inc(self, instr, k=1):
        instr.then_inc(self.h, k)
        self.n += k
        return self.n

    def wait(self, eng):
        eng.wait_ge(self.h, self.n)

    def wait_at(self, eng, v):
        if v > 0:
            eng.wait_ge(self.h, v)


class _ChSem(_Sem):

    def inc(self, instr, k=1):
        return self.n


class _Ch:

    def __init__(self, eng, sem):
        object.__setattr__(self, "_e", eng)
        object.__setattr__(self, "_s", sem)

    def wait_ge(self, h, v):
        self._e.wait_ge(h, v)

    def __getattr__(self, nm):
        f = getattr(self._e, nm)
        if nm.startswith("dma") or not callable(f):
            return f
        s = self._s
        e = self._e

        def g(*a, **k):
            if s.n:
                e.wait_ge(s.h, s.n)
            r = f(*a, **k)
            r.then_inc(s.h, 1)
            s.n += 1
            return r
        return g


def _build(debug=False):
    nc = bass.Bass("TRN2", target_bir_lowering=False)
    dt = nc.dram_tensor
    x0l = dt("x0l", [TLOC, D], F32, kind="ExternalInput").ap()
    wq_t = dt("wq_t", [3 * NH, CI, 128, 128], F32, kind="ExternalInput").ap()
    wout_t = dt("wout_t", [NH, CI, 128, 128], F32, kind="ExternalInput").ap()
    gate_t = dt("gate_t", [CI, CI, 128, 128], F32, kind="ExternalInput").ap()
    agg_t = dt("agg_t", [CI, 4, 128, 512], F32, kind="ExternalInput").ap()
    bc_t = dt("bc_t", [CI, 4, 128, 512], F32, kind="ExternalInput").ap()
    d1s_t = dt("d1s_t", [8, 4, 128, 512], F32, kind="ExternalInput").ap()
    d2s_t = dt("d2s_t", [8, 4, 128, 512], F32, kind="ExternalInput").ap()
    cosT_i = dt("cosT", [32, T + 1], F32, kind="ExternalInput").ap()
    sinT_i = dt("sinT", [32, T + 1], F32, kind="ExternalInput").ap()
    maskD_i = dt("maskD", [128, 128], F32, kind="ExternalInput").ap()
    maskO_i = dt("maskO", [128, 128], F32, kind="ExternalInput").ap()
    ident_i = dt("ident", [128, 128], F32, kind="ExternalInput").ap()
    ing_i = dt("ing_c", [128, CI], F32, kind="ExternalInput").ap()
    inb_i = dt("inb_c", [128, CI], F32, kind="ExternalInput").ap()
    og_i = dt("og_c", [128, CI], F32, kind="ExternalInput").ap()
    ob_i = dt("ob_c", [128, CI], F32, kind="ExternalInput").ap()
    gb_i = dt("gb_c", [128, CI], F32, kind="ExternalInput").ap()
    png_i = dt("png_r", [4, D], F32, kind="ExternalInput").ap()
    pnb_i = dt("pnb_r", [4, D], F32, kind="ExternalInput").ap()
    aggb_i = dt("aggb_r", [4, D], F32, kind="ExternalInput").ap()
    bcb_i = dt("bcb_r", [4, D], F32, kind="ExternalInput").ap()
    d1b_i = dt("d1b_r", [4, 1024], F32, kind="ExternalInput").ap()
    d2b_i = dt("d2b_r", [4, D], F32, kind="ExternalInput").ap()
    ohh_i = dt("ohh_r", [1, 4], F32, kind="ExternalInput").ap()
    oh1_i = dt("oh1_r", [4, 1], F32, kind="ExternalInput").ap()
    oT_sph = dt("oT_sph", [NH, 128, OWN + 1], BF16).ap()
    oT_spl = dt("oT_spl", [NH, 128, OWN + 1], BF16).ap()
    oT_tlh = dt("oT_tlh", [NH, 128, 1], BF16).ap()
    oT_tll = dt("oT_tll", [NH, 128, 1], BF16).ap()
    arin = dt("arin", [4, D], F32).ap()
    arout = dt("arout", [4, D], F32).ap()
    x1_sp = dt("x1_sp", [128, CI * OWN], F32).ap()
    y_out = dt("y_out", [OWN, D], F32, kind="ExternalOutput").ap()
    dbg = {}
    if debug:
        for nm, shp in [
            ("dbg_xln", [128, CI * XCOL]), ("dbg_xlnl", [128, CI * XCOL]),
            ("dbg_q", [128, OWN + 1]),
            ("dbg_k", [128, T]), ("dbg_v", [128, T]),
            ("dbg_ot", [128, OWN + 1]), ("dbg_x1", [128, CI * OWN]),
            ("dbg_h1", [128, CI * OWN]), ("dbg_gate", [128, CI * OWN]),
            ("dbg_th", [1, D]), ("dbg_bc", [128, CI]),
            ("dbg_ytail", [128, CI]), ("dbg_otbt", [128, NH]),
        ]:
            dty = (BF16 if nm in ("dbg_xln", "dbg_xlnl", "dbg_otbt")
                   else F32)
            dbg[nm] = dt(nm, shp, dty, kind="ExternalOutput").ap()

    st = ExitStack()
    with st:
        e = st.enter_context
        ident = e(nc.sbuf_tensor("identS", [128, 128], F32))
        onesf = e(nc.sbuf_tensor("onesf", [128, 1], F32))
        ones1r = e(nc.sbuf_tensor("ones1r", [1, 128], F32))
        onesb = e(nc.sbuf_tensor("onesb", [1, 128], BF16))
        epsc = e(nc.sbuf_tensor("epsc", [128, 2], F32))
        colc = {nm: e(nc.sbuf_tensor(nm + "_s", [128, CI], F32))
                for nm in ("ing", "inb", "og", "ob", "gb")}

        S = {}
        for nm in ("ldm", "ldx0a", "ldx0b", "ldwa", "ldwb", "ldwc", "ldo",
                   "ldx1a", "ldx1b", "spl", "pe", "out", "gq", "cc", "ccd"):
            S[nm] = _Sem(st, nc, "s_" + nm)
        S["dbgs"] = _Sem(st, nc, "s_dbgs")
        for nm in ("dve", "sc"):
            S[nm] = _ChSem(st, nc, "s_" + nm)
        ldm = S["ldm"]
        ldx0 = [S["ldx0a"], S["ldx0b"]]
        ldw = [S["ldwa"], S["ldwb"]]
        ldo = S["ldo"]
        ldx1 = [S["ldx1a"], S["ldx1b"]]
        spl, pe, dve, sc, out_s, gq = (S["spl"], S["pe"], S["dve"], S["sc"],
                                       S["out"], S["gq"])
        dbgs = S["dbgs"]

        blk = e(nc.Block())

        @blk.sync
        def _(q):
            for dst, src in [(ident, ident_i),
                             (colc["ing"], ing_i), (colc["inb"], inb_i),
                             (colc["og"], og_i), (colc["ob"], ob_i),
                             (colc["gb"], gb_i)]:
                ldm.inc(q.dma_start(out=dst[:], in_=src), 16)

        @blk.vector
        def _(v):
            v = _Ch(v, dve)
            v.memset(onesf[:], 1.0)
            v.memset(ones1r[:], 1.0)
            v.memset(epsc[:, 0:1], EPS1)
            v.memset(epsc[:, 1:2], EPS2)
            dve.inc(v.memset(onesb[:], 1.0))

        with ExitStack() as p12:
            xlnTh = p12.enter_context(
                nc.sbuf_tensor("xlnTh", [128, CI * XCOL], BF16))
            xlnTl = p12.enter_context(
                nc.sbuf_tensor("xlnTl", [128, CI * XCOL], BF16))

            def xlh(ci, a, b):
                return xlnTh[:, ci * XCOL + a: ci * XCOL + b]

            def xll(ci, a, b):
                return xlnTl[:, ci * XCOL + a: ci * XCOL + b]

            with ExitStack() as ph:
                pe_ = ph.enter_context
                x0tm = [pe_(nc.sbuf_tensor(f"x0tm{i}", [128, D], F32))
                        for i in range(2)]
                xlntm = [pe_(nc.sbuf_tensor(f"xlntm{i}", [128, D], F32))
                         for i in range(2)]
                sqscr = pe_(nc.sbuf_tensor("sqscr", [128, D], F32))
                stat = pe_(nc.sbuf_tensor("stat", [128, 16], F32))
                psT = [pe_(nc.psum_tensor(f"psT{i}", [128, 512], F32))
                       for i in range(2)]

                norm_done = [0, 0]
                trs_done = [0, 0]
                psT_ev = [0, 0]

                for tc in range(TCH):
                    s = tc % 2
                    so = s * 8

                    @blk.sync
                    def _(q, tc=tc, s=s):
                        if tc >= 2:
                            dve.wait_at(q, norm_done[s])
                        ldx0[s].inc(q.dma_start(
                            out=x0tm[s][:],
                            in_=x0l[tc * 128:(tc + 1) * 128, :]), 16)
                    ld_v = ldx0[s].n

                    @blk.scalar
                    def _(c, s=s, so=so, ld_v=ld_v):
                        c = _Ch(c, sc)
                        ldx0[s].wait_at(c, ld_v)
                        sc.inc(c.activation(sqscr[:], x0tm[s][:], AFT.Square,
                                            accum_out=stat[:, so + 1:so + 2]))
                    sq_v = sc.n

                    @blk.vector
                    def _(v, s=s, so=so, ld_v=ld_v, sq_v=sq_v):
                        v = _Ch(v, dve)
                        ldx0[s].wait_at(v, ld_v)
                        v.reduce_sum(stat[:, so:so + 1], x0tm[s][:], axis=AXX)
                        v.tensor_scalar_mul(stat[:, so + 2:so + 3],
                                            stat[:, so:so + 1], 1.0 / D)
                        v.tensor_mul(stat[:, so + 3:so + 4],
                                     stat[:, so + 2:so + 3],
                                     stat[:, so + 2:so + 3])
                        sc.wait_at(v, sq_v)
                        v.tensor_scalar_mul(stat[:, so + 4:so + 5],
                                            stat[:, so + 1:so + 2], 1.0 / D)
                        dve.inc(v.tensor_sub(stat[:, so + 5:so + 6],
                                             stat[:, so + 4:so + 5],
                                             stat[:, so + 3:so + 4]))
                    var_v = dve.n

                    @blk.scalar
                    def _(c, so=so, var_v=var_v):
                        c = _Ch(c, sc)
                        dve.wait_at(c, var_v)
                        sc.inc(c.activation(stat[:, so + 6:so + 7],
                                            stat[:, so + 5:so + 6], AFT.Sqrt,
                                            bias=epsc[:, 0:1]))
                    sd_v = sc.n

                    @blk.vector
                    def _(v, s=s, so=so, sd_v=sd_v, tc=tc):
                        v = _Ch(v, dve)
                        sc.wait_at(v, sd_v)
                        v.reciprocal(stat[:, so + 7:so + 8],
                                     stat[:, so + 6:so + 7])
                        if tc >= 2:
                            pe.wait_at(v, trs_done[s])
                        dve.inc(v.tensor_scalar(xlntm[s][:], x0tm[s][:],
                                                stat[:, so + 2:so + 3],
                                                stat[:, so + 7:so + 8],
                                                AOP.subtract, AOP.mult))
                    norm_done[s] = dve.n

                    for g in range(4):
                        pb = (tc * 4 + g) % 2

                        @blk.tensor
                        def _(t, g=g, pb=pb, s=s, tc=tc):
                            if g == 0:
                                dve.wait_at(t, norm_done[s])
                                ldm.wait(t)
                            dve.wait_at(t, psT_ev[pb])
                            for i4 in range(4):
                                ci = g * 4 + i4
                                mm = t.matmul(
                                    psT[pb][:, i4 * 128:(i4 + 1) * 128],
                                    xlntm[s][:, ci * 128:(ci + 1) * 128],
                                    ident[:], is_transpose=True,
                                    start=(i4 == 0), stop=(i4 == 3))
                            pe.inc(mm)
                        gv = pe.n

                        @blk.scalar
                        def _(c, g=g, pb=pb, tc=tc, gv=gv):
                            c = _Ch(c, sc)
                            pe.wait_at(c, gv)
                            rh = xlnTh[:].rearrange("p (ci t) -> p ci t", ci=CI)
                            s_ = psT[pb][:].rearrange("p (a b) -> p a b", a=4)
                            if tc < 16:
                                sc.inc(c.copy(rh[:, g * 4:(g + 1) * 4,
                                                 tc * 128:(tc + 1) * 128], s_))
                            else:
                                sc.inc(c.copy(rh[:, g * 4:(g + 1) * 4,
                                                 2048:2049],
                                              s_[:, :, 127:128]))
                        hi_v = sc.n

                        @blk.vector
                        def _(v, g=g, pb=pb, tc=tc, hi_v=hi_v):
                            v = _Ch(v, dve)
                            sc.wait_at(v, hi_v)
                            rh = xlnTh[:].rearrange("p (ci t) -> p ci t", ci=CI)
                            rl = xlnTl[:].rearrange("p (ci t) -> p ci t", ci=CI)
                            s_ = psT[pb][:].rearrange("p (a b) -> p a b", a=4)
                            if tc < 16:
                                sl_h = rh[:, g * 4:(g + 1) * 4,
                                          tc * 128:(tc + 1) * 128]
                                sl_l = rl[:, g * 4:(g + 1) * 4,
                                          tc * 128:(tc + 1) * 128]
                                dve.inc(v.tensor_sub(sl_l, s_, sl_h))
                            else:
                                dve.inc(v.tensor_sub(
                                    rl[:, g * 4:(g + 1) * 4, 2048:2049],
                                    s_[:, :, 127:128],
                                    rh[:, g * 4:(g + 1) * 4, 2048:2049]))
                        psT_ev[pb] = dve.n
                    trs_done[s] = pe.n
                entry_ev = sc.n
                if debug:
                    @blk.gpsimd
                    def _(g):
                        sc.wait_at(g, entry_ev)
                        dve.wait_at(g, psT_ev[1])
                        dbgs.inc(g.dma_start(out=dbg["dbg_xln"],
                                              in_=xlnTh[:]), 16)
                        dbgs.inc(g.dma_start(out=dbg["dbg_xlnl"],
                                              in_=xlnTl[:]), 16)

            with ExitStack() as ph:
                pe_ = ph.enter_context
                cosT = pe_(nc.sbuf_tensor("cosT_s", [32, T + 1], F32))
                sinT = pe_(nc.sbuf_tensor("sinT_s", [32, T + 1], F32))
                maskD = pe_(nc.sbuf_tensor("maskD_s", [128, 128], F32))
                maskO = pe_(nc.sbuf_tensor("maskO_s", [128, 128], F32))
                wstg = pe_(nc.sbuf_tensor("wstg", [128, 4 * 128], F32))
                wph = [pe_(nc.sbuf_tensor(f"wph{i}", [128, 4 * 128], BF16))
                       for i in range(2)]
                wpl = [pe_(nc.sbuf_tensor(f"wpl{i}", [128, 4 * 128], BF16))
                       for i in range(2)]
                Kh = pe_(nc.sbuf_tensor("Kh", [128, T], F32))
                Vh = pe_(nc.sbuf_tensor("Vh", [128, T], F32))
                Qh = pe_(nc.sbuf_tensor("Qh", [128, OWN + 1], F32))
                scr = pe_(nc.sbuf_tensor("scr", [128, T], F32))
                PT = [pe_(nc.sbuf_tensor(f"PT{i}", [128, T], F32))
                      for i in range(2)]
                PTt = pe_(nc.sbuf_tensor("PTt", [128, 16], F32))
                OTh = pe_(nc.sbuf_tensor("OTh", [128, OWN + 1], F32))
                cvec = pe_(nc.sbuf_tensor("cvec", [1, OWN + 1], F32))
                negcb = scr[0:1, 1040:1553].bitcast(BF16)[:, 0:OWN + 1]
                nct = pe_(nc.sbuf_tensor("nct", [128, 1], F32))
                invs = cvec[0:1, 0:OWN]
                kqv = pe_(nc.sbuf_tensor("kqv", [1, 8], F32))
                psST = pe_(nc.psum_tensor("psST", [128, 2048], F32))
                psO = [pe_(nc.psum_tensor(f"psO{i}", [128, 128], F32))
                       for i in range(2)]
                psSum = pe_(nc.psum_tensor("psSum", [1, OWN], F32))

                @blk.sync
                def _(q):
                    ldx0[0].wait(q)
                    ldx0[1].wait(q)
                    dve.wait(q)
                    sc.wait(q)
                    pe.wait(q)
                    for dst, src in [(cosT, cosT_i), (sinT, sinT_i),
                                     (maskD, maskD_i), (maskO, maskO_i)]:
                        ldm.inc(q.dma_start(out=dst[:], in_=src), 16)
                ldm2 = ldm.n

                wld = {}
                wsplit = {}
                wconsumed = {}

                def load_half(q, hidx):
                    pidx, qq = divmod(hidx, 4)
                    src = wq_t[pidx, qq * 4:(qq + 1) * 4].transpose([1, 0, 2])
                    if hidx >= 1:
                        dve.wait_at(q, wsplit[hidx - 1])
                    ldw[hidx % 2].inc(
                        q.dma_start(
                            out=wstg[:].rearrange("p (ci f) -> p ci f", ci=4),
                            in_=src), 16)
                    wld[hidx] = ldw[hidx % 2].n

                def split_half(hidx):
                    @blk.scalar
                    def _(c, hidx=hidx):
                        c = _Ch(c, sc)
                        ldw[hidx % 2].wait_at(c, wld[hidx])
                        if hidx >= 2:
                            pe.wait_at(c, wconsumed[hidx - 2])
                        sc.inc(c.copy(wph[hidx % 2][:], wstg[:]))
                    cast_v = sc.n

                    @blk.vector
                    def _(v, hidx=hidx, cast_v=cast_v):
                        v = _Ch(v, dve)
                        sc.wait_at(v, cast_v)
                        dve.inc(v.tensor_sub(wpl[hidx % 2][:], wstg[:],
                                             wph[hidx % 2][:]))
                    wsplit[hidx] = dve.n

                @blk.sync
                def _(q):
                    load_half(q, 0)
                split_half(0)

                @blk.sync
                def _(q):
                    load_half(q, 1)
                split_half(1)

                @blk.sync
                def _(q):
                    load_half(q, 2)

                prev = dict(invrep_ev=0, spl=0, onorm=0, invs=0,
                            pv6=0, pv7=0, rope=0, tpv=0)

                QSL = ((0, 512), (512, 512), (2048, 1))

                NQT = 12 * NH
                for h in range(NH):
                    for hh in range(4):
                        hidx = 12 * h + hh

                        @blk.tensor
                        def _(t, h=h, hh=hh, hidx=hidx):
                            dve.wait_at(t, wsplit[hidx])
                            if hh == 0:
                                dve.wait_at(t, psT_ev[1] if not h else 0)
                                sc.wait_at(t, prev["invrep_ev"] if h
                                           else entry_ev)
                            for c4 in range(4):
                                ci = hh * 4 + c4
                                wbh = wph[hidx % 2][:, c4 * 128:(c4 + 1) * 128]
                                wbl = wpl[hidx % 2][:, c4 * 128:(c4 + 1) * 128]
                                for sl, (t0, tw) in enumerate(QSL):
                                    reg = psST[:, sl * 512:sl * 512 + tw]
                                    t.matmul(reg, wbh, xlh(ci, t0, t0 + tw),
                                             start=(ci == 0), stop=False)
                                    t.matmul(reg, wbh, xll(ci, t0, t0 + tw),
                                             start=False, stop=False)
                                    mm = t.matmul(
                                        reg, wbl, xlh(ci, t0, t0 + tw),
                                        start=False, stop=(ci == CI - 1))
                            pe.inc(mm)
                        wconsumed[hidx] = pe.n

                        if hidx + 2 < NQT:
                            split_half(hidx + 2)

                        @blk.sync
                        def _(q, hidx=hidx):
                            if hidx + 3 < NQT:
                                load_half(q, hidx + 3)
                    q_mm = pe.n

                    @blk.scalar
                    def _(c, q_mm=q_mm, h=h):
                        c = _Ch(c, sc)
                        pe.wait_at(c, q_mm)
                        if debug and h == 1:
                            dbgs.wait_at(c, dbg0_v)
                        for sl, (t0, tw) in enumerate(((0, 512), (512, 512),
                                                       (1024, 1))):
                            sc.inc(c.activation(
                                Qh[:, t0:t0 + tw],
                                psST[:, sl * 512:sl * 512 + tw],
                                AFT.Copy, scale=QSCALE))
                    q_ev = sc.n

                    for hh in range(4):
                        hidx = 12 * h + 4 + hh

                        @blk.tensor
                        def _(t, hh=hh, hidx=hidx, q_ev=q_ev):
                            dve.wait_at(t, wsplit[hidx])
                            for sl in (3, 0, 1, 2):
                                if hh == 0 and sl != 3:
                                    sc.wait_at(t, q_ev - 2 + sl)
                                for c4 in range(4):
                                    ci = hh * 4 + c4
                                    wbh = wph[hidx % 2][:, c4 * 128:
                                                        (c4 + 1) * 128]
                                    wbl = wpl[hidx % 2][:, c4 * 128:
                                                        (c4 + 1) * 128]
                                    reg = psST[:, sl * 512:(sl + 1) * 512]
                                    a0, a1 = sl * 512, (sl + 1) * 512
                                    t.matmul(reg, wbh, xlh(ci, a0, a1),
                                             start=(ci == 0), stop=False)
                                    t.matmul(reg, wbh, xll(ci, a0, a1),
                                             start=False, stop=False)
                                    mm = t.matmul(
                                        reg, wbl, xlh(ci, a0, a1),
                                        start=False, stop=(ci == CI - 1))
                            pe.inc(mm)
                        wconsumed[hidx] = pe.n

                        if hidx + 2 < NQT:
                            split_half(hidx + 2)

                        @blk.sync
                        def _(q, hidx=hidx):
                            if hidx + 3 < NQT:
                                load_half(q, hidx + 3)
                    k_mm = pe.n

                    @blk.scalar
                    def _(c, k_mm=k_mm):
                        c = _Ch(c, sc)
                        pe.wait_at(c, k_mm)
                        for sl in range(4):
                            sc.inc(c.copy(Kh[:, sl * 512:(sl + 1) * 512],
                                          psST[:, sl * 512:(sl + 1) * 512]))
                    k_ev = sc.n

                    ksw = PT[0][0:32, 0:T]
                    qsw = PT[1][0:32, 0:OWN + 1]

                    @blk.scalar
                    def _(c, h=h, q_ev=q_ev):
                        c = _Ch(c, sc)
                        sc.wait_at(c, sc.n)
                        ldm.wait_at(c, ldm2)
                        dve.wait_at(c, prev["onorm"])
                        pe.wait_at(c, prev["pv7"])
                        gq.inc(c.dma_start(out=qsw[0:16, :],
                                           in_=Qh[16:32, 0:OWN + 1]), 16)
                        gq.inc(c.dma_start(out=qsw[16:32, :],
                                           in_=Qh[0:16, 0:OWN + 1]), 16)
                        gq.inc(c.dma_start(out=ksw[0:16, :],
                                           in_=Kh[16:32, 0:T]), 16)
                        gq.inc(c.dma_start(out=ksw[16:32, :],
                                           in_=Kh[0:16, 0:T]), 16)
                    sw_v = gq.n

                    @blk.vector
                    def _(v, sw_v=sw_v):
                        v = _Ch(v, dve)
                        gq.wait_at(v, sw_v)
                        v.tensor_mul(Qh[0:32, 0:OWN], Qh[0:32, 0:OWN],
                                     cosT[:, 0:OWN])
                        v.tensor_mul(qsw[:, 0:OWN], qsw[:, 0:OWN],
                                     sinT[:, 0:OWN])
                        v.tensor_add(Qh[0:32, 0:OWN], Qh[0:32, 0:OWN],
                                     qsw[:, 0:OWN])
                        v.tensor_mul(Qh[0:32, OWN:OWN + 1],
                                     Qh[0:32, OWN:OWN + 1], cosT[:, T:T + 1])
                        v.tensor_mul(qsw[:, OWN:OWN + 1], qsw[:, OWN:OWN + 1],
                                     sinT[:, T:T + 1])
                        v.tensor_add(Qh[0:32, OWN:OWN + 1],
                                     Qh[0:32, OWN:OWN + 1],
                                     qsw[:, OWN:OWN + 1])
                        v.tensor_mul(Kh[0:32, :], Kh[0:32, :], cosT[:, 0:T])
                        v.tensor_mul(ksw[:], ksw[:], sinT[:, 0:T])
                        dve.inc(v.tensor_add(Kh[0:32, :], Kh[0:32, :],
                                             ksw[:]))
                    prev["rope"] = rope_v = dve.n

                    @blk.scalar
                    def _(c, rope_v=rope_v):
                        c = _Ch(c, sc)
                        dve.wait_at(c, rope_v)
                        sc.inc(c.activation(PT[1][:], Kh[:], AFT.Square))
                    ksq_v = sc.n

                    for hh in range(4):
                        hidx = 12 * h + 8 + hh

                        @blk.tensor
                        def _(t, hh=hh, hidx=hidx, k_ev=k_ev):
                            dve.wait_at(t, wsplit[hidx])
                            for sl in range(4):
                                if hh == 0:
                                    sc.wait_at(t, k_ev - 3 + sl)
                                for c4 in range(4):
                                    ci = hh * 4 + c4
                                    wbh = wph[hidx % 2][:, c4 * 128:
                                                        (c4 + 1) * 128]
                                    wbl = wpl[hidx % 2][:, c4 * 128:
                                                        (c4 + 1) * 128]
                                    reg = psST[:, sl * 512:(sl + 1) * 512]
                                    a0, a1 = sl * 512, (sl + 1) * 512
                                    t.matmul(reg, wbh, xlh(ci, a0, a1),
                                             start=(ci == 0), stop=False)
                                    t.matmul(reg, wbh, xll(ci, a0, a1),
                                             start=False, stop=False)
                                    mm = t.matmul(
                                        reg, wbl, xlh(ci, a0, a1),
                                        start=False, stop=(ci == CI - 1))
                            pe.inc(mm)
                        wconsumed[hidx] = pe.n

                        if hidx + 2 < NQT:
                            split_half(hidx + 2)

                        @blk.sync
                        def _(q, hidx=hidx):
                            if hidx + 3 < NQT:
                                load_half(q, hidx + 3)
                    v_mm = pe.n

                    @blk.scalar
                    def _(c, v_mm=v_mm):
                        c = _Ch(c, sc)
                        pe.wait_at(c, v_mm)
                        dve.wait_at(c, prev["onorm"])
                        for sl in range(4):
                            sc.inc(c.copy(scr[:, sl * 512:(sl + 1) * 512],
                                          psST[:, sl * 512:(sl + 1) * 512]))
                    vd_ev = sc.n

                    @blk.tensor
                    def _(t, ksq_v=ksq_v):
                        sc.wait_at(t, ksq_v)
                        t.matmul(psSum[0:1, 0:512], onesf[:],
                                 PT[1][:, 0:512], start=True, stop=True)
                        pe.inc(t.matmul(psSum[0:1, 512:1024], onesf[:],
                                        PT[1][:, 512:1024],
                                        start=True, stop=True))
                    ksumA_v = pe.n

                    @blk.vector
                    def _(v, ksumA_v=ksumA_v):
                        v = _Ch(v, dve)
                        pe.wait_at(v, ksumA_v)
                        dve.inc(v.reduce_max(kqv[:, 2:3], psSum[0:1, 0:1024],
                                             axis=AXX))
                    kmaxA_v = dve.n

                    v_ev = [0] * 4
                    for tg in range(4):
                        @blk.tensor
                        def _(t, tg=tg, vd_ev=vd_ev):
                            if tg == 0:
                                sc.wait_at(t, vd_ev)
                            else:
                                sc.wait_at(t, v_ev[tg - 1])
                            for t4 in range(4):
                                tc = tg * 4 + t4
                                mm = t.matmul(
                                    psST[:, t4 * 128:(t4 + 1) * 128],
                                    scr[:, tc * 128:(tc + 1) * 128],
                                    ident[:], is_transpose=True,
                                    start=(t4 == 0), stop=(t4 == 3))
                            pe.inc(mm)
                        vtr = pe.n

                        @blk.scalar
                        def _(c, tg=tg, vtr=vtr):
                            c = _Ch(c, sc)
                            pe.wait_at(c, vtr)
                            if tg == 0:
                                spl.wait_at(c, prev["spl"])
                            sc.inc(c.copy(
                                Vh[:].rearrange("p (a b) -> p a b", a=16)
                                [:, tg * 4:(tg + 1) * 4, :],
                                psST[:, 0:512]
                                .rearrange("p (a b) -> p a b", a=4)))
                        v_ev[tg] = sc.n



                    @blk.tensor
                    def _(t, kmaxA_v=kmaxA_v):
                        dve.wait_at(t, kmaxA_v)
                        t.matmul(psSum[0:1, 0:512], onesf[:],
                                 PT[1][:, 1024:1536], start=True, stop=True)
                        pe.inc(t.matmul(psSum[0:1, 512:1024], onesf[:],
                                        PT[1][:, 1536:2048],
                                        start=True, stop=True))
                    ksumB_v = pe.n

                    @blk.vector
                    def _(v, ksumB_v=ksumB_v):
                        v = _Ch(v, dve)
                        pe.wait_at(v, ksumB_v)
                        v.reduce_max(kqv[:, 3:4], psSum[0:1, 0:1024],
                                     axis=AXX)
                        dve.inc(v.tensor_tensor(kqv[:, 0:1], kqv[:, 2:3],
                                                kqv[:, 3:4], AOP.max))
                    kmaxB_v = dve.n

                    @blk.scalar
                    def _(c, h=h, ksumA_v=ksumA_v):
                        c = _Ch(c, sc)
                        pe.wait_at(c, ksumA_v)
                        if h:
                            pe.wait_at(c, prev["tpv"])
                        c.activation(PT[1][:, 0:1024], Qh[:, 0:1024],
                                     AFT.Square)
                        sc.inc(c.activation(PTt[:, 0:1], Qh[:, OWN:OWN + 1],
                                            AFT.Square))
                    qsq_v = sc.n

                    @blk.tensor
                    def _(t, qsq_v=qsq_v, kmaxB_v=kmaxB_v):
                        sc.wait_at(t, qsq_v)
                        dve.wait_at(t, kmaxB_v)
                        t.matmul(psSum[0:1, 0:512], onesf[:],
                                 PT[1][:, 0:512], start=True, stop=True)
                        t.matmul(psSum[0:1, 512:1024], onesf[:],
                                 PT[1][:, 512:1024], start=True, stop=True)
                        pe.inc(t.matmul(psO[0][0:1, 0:1], onesf[:],
                                        PTt[:, 0:1], start=True, stop=True))
                    qsum_v = pe.n

                    @blk.vector
                    def _(v, qsum_v=qsum_v):
                        v = _Ch(v, dve)
                        pe.wait_at(v, qsum_v)
                        v.tensor_single_scalar(cvec[0:1, 0:OWN],
                                               psSum[0:1, 0:OWN],
                                               kqv[:, 0:1], AOP.mult)
                        dve.inc(v.tensor_single_scalar(
                            cvec[0:1, OWN:OWN + 1], psO[0][0:1, 0:1],
                            kqv[:, 0:1], AOP.mult))
                    c2_v = dve.n

                    @blk.scalar
                    def _(c, c2_v=c2_v):
                        c = _Ch(c, sc)
                        dve.wait_at(c, c2_v)
                        sc.inc(c.activation(cvec[:], cvec[:], AFT.Sqrt))
                    c_v = sc.n

                    @blk.vector
                    def _(v, c_v=c_v):
                        v = _Ch(v, dve)
                        sc.wait_at(v, c_v)
                        dve.inc(v.tensor_scalar_mul(negcb, cvec[:], -1.0))
                    negc_v = dve.n

                    exp_v = {}
                    pv_v = {}
                    pso_ev = {}
                    for j in range(9):
                        if j < 8:
                            kcs = list(range(j + 1)) + list(range(8, 9 + j))
                            nb = len(kcs)

                            @blk.tensor
                            def _(t, j=j, kcs=kcs, nb=nb,
                                  rope_v=rope_v, negc_v=negc_v):
                                if j == 0:
                                    dve.wait_at(t, rope_v)
                                    sc.wait_at(t, v_ev[3])
                                else:
                                    sc.wait_at(t, exp_v[j - 1])
                                for bi, kc in enumerate(kcs):
                                    t.matmul(
                                        psST[:, bi * 128:(bi + 1) * 128],
                                        Kh[:, kc * 128:(kc + 1) * 128],
                                        Qh[:, j * 128:(j + 1) * 128],
                                        start=(bi % 4 == 0), stop=False)
                                if j == 0:
                                    dve.wait_at(t, negc_v)
                                for bi in range(nb):
                                    mm = t.matmul(
                                        psST[:, bi * 128:(bi + 1) * 128],
                                        onesb[:],
                                        negcb[0:1, j * 128:(j + 1) * 128],
                                        start=False,
                                        stop=(bi % 4 == 3 or bi == nb - 1))
                                pe.inc(mm)
                            st_v = pe.n

                            @blk.vector
                            def _(v, j=j, nb=nb, st_v=st_v):
                                v = _Ch(v, dve)
                                pe.wait_at(v, st_v)
                                v.tensor_tensor(
                                    psST[:, j * 128:(j + 1) * 128],
                                    psST[:, j * 128:(j + 1) * 128],
                                    maskD[:], AOP.add)
                                dve.inc(v.tensor_tensor(
                                    psST[:, (nb - 1) * 128:nb * 128],
                                    psST[:, (nb - 1) * 128:nb * 128],
                                    maskO[:], AOP.add))
                            mask_v = dve.n

                            @blk.scalar
                            def _(c, j=j, nb=nb, mask_v=mask_v):
                                c = _Ch(c, sc)
                                dve.wait_at(c, mask_v)
                                if j >= 2:
                                    pe.wait_at(c, pv_v[j - 2])
                                elif j == 0:
                                    pe.wait_at(c, prev["pv6"])
                                else:
                                    pe.wait_at(c, prev["pv7"])
                                sc.inc(c.activation(PT[j % 2][:, 0:nb * 128],
                                                    psST[:, 0:nb * 128],
                                                    AFT.Exp))
                            exp_v[j] = sc.n

                        if j >= 1:
                            jj = j - 1
                            kcs2 = list(range(jj + 1)) + list(range(8, 9 + jj))
                            nb2 = len(kcs2)

                            @blk.tensor
                            def _(t, h=h, jj=jj, kcs2=kcs2, nb2=nb2):
                                sc.wait_at(t, exp_v[jj])
                                sc.wait_at(t, pso_ev.get(jj - 2, 0))
                                if jj == 0 and h:
                                    dve.wait_at(t, prev["invs"])
                                s_start = jj in (0, 4)
                                s_stop = jj in (3, 7)
                                for bi in range(nb2):
                                    t.matmul(
                                        psO[jj % 2][:, 0:128],
                                        Vh[:, kcs2[bi] * 128:
                                           (kcs2[bi] + 1) * 128],
                                        PT[jj % 2][:, bi * 128:(bi + 1) * 128],
                                        start=(bi == 0), stop=(bi == nb2 - 1))
                                for bi in range(nb2):
                                    mm = t.matmul(
                                        psSum[0:1, jj * 128:(jj + 1) * 128],
                                        onesf[:],
                                        PT[jj % 2][:, bi * 128:(bi + 1) * 128],
                                        start=(s_start and bi == 0),
                                        stop=(s_stop and bi == nb2 - 1))
                                pe.inc(mm)
                            pv_v[jj] = pe.n

                            @blk.scalar
                            def _(c, h=h, jj=jj):
                                c = _Ch(c, sc)
                                pe.wait_at(c, pv_v[jj])
                                if jj == 0 and h:
                                    spl.wait_at(c, prev["spl"])
                                sc.inc(c.copy(OTh[:, jj * 128:(jj + 1) * 128],
                                              psO[jj % 2][:]))
                            pso_ev[jj] = sc.n

                        if j == 1:
                            @blk.tensor
                            def _(t, negc_v=negc_v):
                                dve.wait_at(t, negc_v)
                                pe.inc(t.matmul(psST[:, 1920:1921], onesb[:],
                                                negcb[0:1, OWN:OWN + 1],
                                                start=True, stop=True))
                            nctp_v = pe.n

                            @blk.scalar
                            def _(c, nctp_v=nctp_v):
                                c = _Ch(c, sc)
                                pe.wait_at(c, nctp_v)
                                sc.inc(c.copy(nct[:], psST[:, 1920:1921]))
                            nct_v = sc.n
                    prev["pv6"] = pv_v[6]
                    prev["pv7"] = pv_v[7]

                    @blk.tensor
                    def _(t):
                        sc.wait_at(t, exp_v[7])
                        for kc in range(16):
                            mm = t.matmul(psST[:, kc:kc + 1],
                                          Kh[:, kc * 128:(kc + 1) * 128],
                                          Qh[:, OWN:OWN + 1],
                                          start=(kc == 0), stop=(kc == 15))
                        pe.inc(mm)
                    tst_v = pe.n

                    @blk.scalar
                    def _(c, tst_v=tst_v):
                        c = _Ch(c, sc)
                        pe.wait_at(c, tst_v)
                        sc.inc(c.activation(PTt[:], psST[:, 0:16], AFT.Exp,
                                            bias=nct[:]))
                    texp_v = sc.n

                    @blk.tensor
                    def _(t, texp_v=texp_v):
                        sc.wait_at(t, texp_v)
                        sc.wait_at(t, pso_ev[7])
                        for kc in range(16):
                            t.matmul(psO[0][:, 0:1],
                                     Vh[:, kc * 128:(kc + 1) * 128],
                                     PTt[:, kc:kc + 1],
                                     start=(kc == 0), stop=(kc == 15))
                        for kc in range(16):
                            mm2 = t.matmul(psO[1][0:1, 0:1], onesf[:],
                                           PTt[:, kc:kc + 1],
                                           start=(kc == 0), stop=(kc == 15))
                        pe.inc(mm2)
                    tpv_v = pe.n
                    prev["tpv"] = tpv_v

                    @blk.vector
                    def _(v, tpv_v=tpv_v):
                        v = _Ch(v, dve)
                        pe.wait_at(v, tpv_v)
                        v.reciprocal(invs, psSum[0:1, :])
                        dve.inc(v.reciprocal(kqv[:, 1:2], psO[1][0:1, 0:1]))
                    prev["invs"] = invs_v = dve.n

                    @blk.tensor
                    def _(t, invs_v=invs_v):
                        dve.wait_at(t, invs_v)
                        for sl in range(2):
                            t.matmul(psST[:, sl * 512:(sl + 1) * 512],
                                     ones1r[:],
                                     invs[0:1, sl * 512:(sl + 1) * 512]
                                     if False else
                                     cvec[0:1, sl * 512:(sl + 1) * 512],
                                     start=True, stop=True)
                        mm = t.matmul(psST[:, 1024:1025], ones1r[:],
                                      kqv[0:1, 1:2], start=True, stop=True)
                        pe.inc(mm)
                    irep_v = pe.n

                    @blk.scalar
                    def _(c, irep_v=irep_v):
                        c = _Ch(c, sc)
                        pe.wait_at(c, irep_v)
                        sc.inc(c.copy(scr[:, 0:1025], psST[:, 0:1025]))
                    prev["invrep_ev"] = irep_ev = sc.n

                    @blk.vector
                    def _(v, irep_ev=irep_ev):
                        v = _Ch(v, dve)
                        sc.wait_at(v, irep_ev)
                        v.tensor_tensor(OTh[:, 0:OWN], OTh[:, 0:OWN],
                                        scr[:, 0:OWN], AOP.mult)
                        dve.inc(v.tensor_tensor(OTh[:, OWN:OWN + 1],
                                                psO[0][:, 0:1],
                                                scr[:, OWN:OWN + 1], AOP.mult))
                    prev["onorm"] = onorm_v = dve.n

                    if debug and h == 0:
                        @blk.gpsimd
                        def _(g, onorm_v=onorm_v):
                            dve.wait_at(g, onorm_v)
                            dbgs.inc(g.dma_start(out=dbg["dbg_q"],
                                                  in_=Qh[:]), 16)
                            dbgs.inc(g.dma_start(out=dbg["dbg_k"],
                                                  in_=Kh[:]), 16)
                            dbgs.inc(g.dma_start(out=dbg["dbg_v"],
                                                  in_=Vh[:]), 16)
                            dbgs.inc(g.dma_start(out=dbg["dbg_ot"],
                                                  in_=OTh[:]), 16)
                        dbg0_v = dbgs.n
                    OThh = Vh[:, 0:513].bitcast(BF16)[:, 0:OWN + 1]
                    OThl = Vh[:, 1024:1537].bitcast(BF16)[:, 0:OWN + 1]

                    @blk.scalar
                    def _(c, h=h, onorm_v=onorm_v):
                        c = _Ch(c, sc)
                        dve.wait_at(c, onorm_v)
                        if debug and h == 0:
                            dbgs.wait_at(c, dbg0_v)
                        sc.inc(c.copy(OThh, OTh[:]))
                    othi_v = sc.n

                    @blk.vector
                    def _(v, othi_v=othi_v):
                        v = _Ch(v, dve)
                        sc.wait_at(v, othi_v)
                        dve.inc(v.tensor_sub(OThl, OTh[:], OThh))
                    otlo_v = dve.n

                    @blk.gpsimd
                    def _(g, h=h, otlo_v=otlo_v):
                        dve.wait_at(g, otlo_v)
                        spl.inc(g.dma_start(out=oT_sph[h], in_=OThh), 16)
                        spl.inc(g.dma_start(out=oT_spl[h], in_=OThl), 16)
                        spl.inc(g.dma_start(out=oT_tlh[h],
                                            in_=OThh[:, OWN:OWN + 1]), 16)
                        spl.inc(g.dma_start(out=oT_tll[h],
                                            in_=OThl[:, OWN:OWN + 1]), 16)
                    prev["spl"] = spl.n

                heads_spl = spl.n

        with ExitStack() as p34:
            h1sb = p34.enter_context(
                nc.sbuf_tensor("h1sb", [128, CI * OWN], F32))
            bccol = p34.enter_context(nc.sbuf_tensor("bccol", [128, CI], F32))
            xnc = p34.enter_context(nc.sbuf_tensor("xnc", [128, CI], F32))

            with ExitStack() as ph:
                pe_ = ph.enter_context
                otbh = pe_(nc.sbuf_tensor("otbh", [128, NH * 512], BF16))
                otbl = pe_(nc.sbuf_tensor("otbl", [128, NH * 512], BF16))
                otbth = pe_(nc.sbuf_tensor("otbth", [128, NH], BF16))
                otbtl = pe_(nc.sbuf_tensor("otbtl", [128, NH], BF16))
                ottmp = pe_(nc.sbuf_tensor("ottmp", [16, 256], BF16))
                identb = pe_(nc.sbuf_tensor("identb", [16, 16], BF16))
                woutb = [pe_(nc.sbuf_tensor(f"woutb{i}", [128, 7 * 128], F32))
                         for i in range(2)]
                wpoh = [pe_(nc.sbuf_tensor(f"wpoh{i}", [128, 7 * 128], BF16))
                        for i in range(4)]
                wpol = [pe_(nc.sbuf_tensor(f"wpol{i}", [128, 7 * 128], BF16))
                        for i in range(4)]
                ytail = pe_(nc.sbuf_tensor("ytail", [128, CI], F32))
                x1sb = pe_(nc.sbuf_tensor("x1sb", [128, CI * OWN], F32))
                x0tm2 = [pe_(nc.sbuf_tensor(f"x0tm2{i}", [128, D], F32))
                         for i in range(2)]
                sqs2 = pe_(nc.sbuf_tensor("sqs2", [128, 1024], F32))
                x0col = pe_(nc.sbuf_tensor("x0col", [128, CI], F32))
                x1col = pe_(nc.sbuf_tensor("x1col", [128, CI], F32))
                tsc = pe_(nc.sbuf_tensor("tsc", [1, 8], F32))
                colsc = pe_(nc.sbuf_tensor("colsc", [128, 2], F32))
                psY = pe_(nc.psum_tensor("psY", [128, 3584], F32))
                psA = pe_(nc.psum_tensor("psA", [128, 512], F32))

                rounds = [list(range(0, 7)), list(range(7, 14)), [14, 15]]
                seq = [(t2, r2, f2) for t2 in range(2) for r2 in range(3)
                       for f2 in range(NH)]
                wout_ld = {}
                wout_sp = {}
                wout_cons = {}

                def load_wout(q, key):
                    tt_, ri_, f_ = key
                    cis = rounds[ri_]
                    sb = woutb[f_ % 2]
                    i = seq.index(key)
                    if i >= 1:
                        dve.wait_at(q, wout_sp.get(seq[i - 1], 0))
                    src = wout_t[f_, cis[0]:cis[0] + len(cis)].transpose(
                        [1, 0, 2])
                    ldw[f_ % 2].inc(
                        q.dma_start(out=sb[:, 0:len(cis) * 128]
                                    .rearrange("p (a b) -> p a b",
                                               a=len(cis)),
                                    in_=src), 16)
                    wout_ld[key] = ldw[f_ % 2].n

                def split_wout(key):
                    tt_, ri_, f_ = key
                    nco = len(rounds[ri_]) * 128
                    i = seq.index(key)

                    @blk.scalar
                    def _(c, key=key, f_=f_, nco=nco, i=i):
                        c = _Ch(c, sc)
                        ldw[f_ % 2].wait_at(c, wout_ld[key])
                        if i >= 4:
                            pe.wait_at(c, wout_cons.get(seq[i - 4], 0))
                        sc.inc(c.copy(wpoh[i % 4][:, 0:nco],
                                      woutb[f_ % 2][:, 0:nco]))
                    cast_v = sc.n

                    @blk.vector
                    def _(v, key=key, i=i, f_=f_, nco=nco, cast_v=cast_v):
                        v = _Ch(v, dve)
                        sc.wait_at(v, cast_v)
                        dve.inc(v.tensor_sub(wpol[i % 4][:, 0:nco],
                                             woutb[f_ % 2][:, 0:nco],
                                             wpoh[i % 4][:, 0:nco]))
                    wout_sp[key] = dve.n

                @blk.sync
                def _(q):
                    spl.wait_at(q, heads_spl)
                    ldw[0].wait_at(q, ldw[0].n)
                    ldw[1].wait_at(q, ldw[1].n)
                    dve.wait(q)
                    sc.wait(q)
                    pe.wait(q)
                    gq.wait(q)
                    dbgs.wait(q)
                    for f in range(NH):
                        ldo.inc(q.dma_start(out=otbh[:, f * 512:(f + 1) * 512],
                                            in_=oT_sph[f][:, 0:512]), 16)
                        ldo.inc(q.dma_start(out=otbl[:, f * 512:(f + 1) * 512],
                                            in_=oT_spl[f][:, 0:512]), 16)
                    load_wout(q, seq[0])
                otb_ld0 = ldo.n

                @blk.sync
                def _(q):
                    ldm.inc(q.dma_start(
                        out=ottmp[:, 0:128],
                        in_=bass.AP(oT_tlh.tensor, 0, [[128, 16], [1, 128]])),
                        16)
                    ldm.inc(q.dma_start(
                        out=ottmp[:, 128:256],
                        in_=bass.AP(oT_tll.tensor, 0, [[128, 16], [1, 128]])),
                        16)
                tail_ld = ldm.n

                @blk.scalar
                def _(c):
                    c = _Ch(c, sc)
                    sc.inc(c.copy(identb[:], ident[0:16, 0:16]))
                idb_v = sc.n

                @blk.tensor
                def _(t):
                    ldm.wait_at(t, tail_ld)
                    sc.wait_at(t, idb_v)
                    t.matmul(psA[:, 32:40].bitcast(BF16), ottmp[:, 0:128],
                             identb[:], is_transpose=True, start=True,
                             stop=False)
                    pe.inc(t.matmul(psA[:, 40:48].bitcast(BF16),
                                    ottmp[:, 128:256],
                                    identb[:], is_transpose=True,
                                    start=False, stop=True))
                ttr_v = pe.n

                @blk.scalar
                def _(c):
                    c = _Ch(c, sc)
                    pe.wait_at(c, ttr_v)
                    c.copy(otbth[:], psA[:, 32:40].bitcast(BF16))
                    sc.inc(c.copy(otbtl[:], psA[:, 40:48].bitcast(BF16)))
                tail_ev = sc.n
                split_wout(seq[0])

                @blk.sync
                def _(q):
                    load_wout(q, seq[1])
                split_wout(seq[1])

                @blk.sync
                def _(q):
                    load_wout(q, seq[2])
                split_wout(seq[2])

                @blk.sync
                def _(q):
                    load_wout(q, seq[3])
                split_wout(seq[3])

                @blk.sync
                def _(q):
                    load_wout(q, seq[4])

                otb_ld = {0: otb_ld0}
                x0ld = {}
                x0tr = {}

                @blk.sync
                def _(g):
                    pe.wait_at(g, pe.n)
                    sc.wait_at(g, sc.n)
                    dve.wait_at(g, dve.n)
                    for k in range(2):
                        ldx0[k % 2].inc(g.dma_start(
                            out=x0tm2[k][:],
                            in_=x0l[k * 128:(k + 1) * 128, :]), 16)
                        x0ld[k] = ldx0[k % 2].n
                if debug:
                    @blk.gpsimd
                    def _(g):
                        sc.wait_at(g, tail_ev)
                        dbgs.inc(g.dma_start(out=dbg["dbg_otbt"],
                                             in_=otbth[:]), 16)
                ytail_ev = 0
                rnd_ev = [0]

                for tt in range(2):
                    for ri, cis in enumerate(rounds):
                        for f in range(NH):
                            key = (tt, ri, f)

                            @blk.tensor
                            def _(t, key=key, cis=cis, tt=tt, ri=ri, f=f):
                                i = seq.index(key)
                                dve.wait_at(t, wout_sp[key])
                                ldo.wait_at(t, otb_ld[tt])
                                if f == 0:
                                    sc.wait_at(t, rnd_ev[0])
                                oh = otbh[:, f * 512:(f + 1) * 512]
                                ol = otbl[:, f * 512:(f + 1) * 512]
                                for k8, ci in enumerate(cis):
                                    wh_ = wpoh[i % 4][:, k8 * 128:
                                                      (k8 + 1) * 128]
                                    wl_ = wpol[i % 4][:, k8 * 128:
                                                      (k8 + 1) * 128]
                                    reg = psY[:, k8 * 512:(k8 + 1) * 512]
                                    t.matmul(reg, wh_, oh,
                                             start=(f == 0), stop=False)
                                    t.matmul(reg, wh_, ol,
                                             start=False, stop=False)
                                    mm = t.matmul(reg, wl_, oh, start=False,
                                                  stop=(f == NH - 1))
                                if tt == 0:
                                    if ri == 0 and f == 0:
                                        sc.wait_at(t, tail_ev)
                                    for k8, ci in enumerate(cis):
                                        wh_ = wpoh[i % 4][:, k8 * 128:
                                                          (k8 + 1) * 128]
                                        wl_ = wpol[i % 4][:, k8 * 128:
                                                          (k8 + 1) * 128]
                                        reg = psA[:, ci:ci + 1]
                                        st_ = (ri == 0 and k8 == 0
                                               and f == 0)
                                        sp_ = (ri == 2 and k8 == 1
                                               and f == NH - 1)
                                        t.matmul(reg, wh_,
                                                 otbth[:, f:f + 1],
                                                 start=st_, stop=False)
                                        t.matmul(reg, wh_,
                                                 otbtl[:, f:f + 1],
                                                 start=False, stop=False)
                                        mm = t.matmul(reg, wl_,
                                                      otbth[:, f:f + 1],
                                                      start=False, stop=sp_)
                                pe.inc(mm)
                            wout_cons[key] = pe.n
                            i_ = seq.index(key)
                            if i_ + 4 < len(seq):
                                split_wout(seq[i_ + 4])

                            @blk.sync
                            def _(q, key=key):
                                i = seq.index(key)
                                if i + 5 < len(seq):
                                    load_wout(q, seq[i + 5])
                                if key == (0, 2, NH - 1):
                                    pe.wait_at(q, pe.n)
                                    for f2 in range(NH):
                                        ldo.inc(q.dma_start(
                                            out=otbh[:, f2 * 512:
                                                     (f2 + 1) * 512],
                                            in_=oT_sph[f2][:, 512:1024]), 16)
                                        ldo.inc(q.dma_start(
                                            out=otbl[:, f2 * 512:
                                                     (f2 + 1) * 512],
                                            in_=oT_spl[f2][:, 512:1024]), 16)
                                    otb_ld[1] = ldo.n
                        rnd_mm = pe.n

                        @blk.scalar
                        def _(c, cis=cis, rnd_mm=rnd_mm, tt=tt):
                            c = _Ch(c, sc)
                            pe.wait_at(c, rnd_mm)
                            for k8, ci in enumerate(cis):
                                sc.inc(c.copy(
                                    x1sb[:, ci * OWN + tt * 512:
                                         ci * OWN + (tt + 1) * 512],
                                    psY[:, k8 * 512:(k8 + 1) * 512]))
                        if tt == 0 and ri == 2:
                            @blk.scalar
                            def _(c):
                                c = _Ch(c, sc)
                                sc.inc(c.copy(ytail[:], psA[:, 0:CI]))
                            ytail_ev = sc.n
                        rnd_ev[0] = sc.n
                    y_ev = sc.n

                    for i in range(4):
                        k = tt * 4 + i
                        x0_v = x0ld[k]

                        for gg in range(4):
                            @blk.tensor
                            def _(t, tt=tt, i=i, k=k, gg=gg, x0_v=x0_v,
                                  y_ev=y_ev):
                                ldx0[k % 2].wait_at(t, x0_v)
                                sc.wait_at(t, y_ev)
                                dve.wait_at(t, dve.n)
                                for i4 in range(4):
                                    ci = gg * 4 + i4
                                    mm = t.matmul(
                                        psA[:, i4 * 128:(i4 + 1) * 128],
                                        x0tm2[k % 2][:, ci * 128:
                                                     (ci + 1) * 128],
                                        ident[:], is_transpose=True,
                                        start=(i4 == 0), stop=(i4 == 3))
                                pe.inc(mm)
                            trv = pe.n

                            @blk.vector
                            def _(v, tt=tt, i=i, gg=gg, trv=trv):
                                v = _Ch(v, dve)
                                pe.wait_at(v, trv)
                                t_off = tt * 512 + i * 128
                                o = x1sb[:].rearrange(
                                    "p (ci t) -> p ci t", ci=CI)[
                                    :, gg * 4:(gg + 1) * 4,
                                    t_off:t_off + 128]
                                dve.inc(v.tensor_tensor(
                                    o,
                                    psA[:].rearrange("p (a b) -> p a b", a=4),
                                    o, AOP.add))
                        x0tr[k] = pe.n

                        @blk.sync
                        def _(g, k=k):
                            if k + 2 < 8:
                                pe.wait_at(g, x0tr[k])
                                ldx0[k % 2].inc(g.dma_start(
                                    out=x0tm2[k % 2][:],
                                    in_=x0l[(k + 2) * 128:(k + 3) * 128, :]),
                                    16)
                                x0ld[k + 2] = ldx0[k % 2].n
                x1_dve = dve.n

                @blk.scalar
                def _(c):
                    c = _Ch(c, sc)
                    sc.wait_at(c, ytail_ev)
                    gq.inc(c.dma_start(
                        out=x0col[:],
                        in_=bass.AP(x0l.tensor, 2175 * D,
                                    [[1, 128], [128, CI]]),
                        allow_slow_non_contiguous=True), 16)
                x0c_v = gq.n

                @blk.vector
                def _(v, x0c_v=x0c_v):
                    v = _Ch(v, dve)
                    gq.wait_at(v, x0c_v)
                    dve.inc(v.tensor_add(x1col[:], x0col[:], ytail[:]))
                x1c_v = dve.n

                @blk.gpsimd
                def _(g):
                    dve.wait_at(g, x1_dve)
                    spl.inc(g.dma_start(out=x1_sp, in_=x1sb[:]), 16)
                    if debug:
                        dbgs.inc(g.dma_start(out=dbg["dbg_x1"],
                                              in_=x1sb[:]), 16)
                        dbgs.inc(g.dma_start(out=dbg["dbg_ytail"],
                                              in_=ytail[:]), 16)
                x1_spl = spl.n

                stat_mm = {}
                pe_base2 = pe.n
                for idx in range(2 * CI):
                    tt, ci = divmod(idx, CI)
                    so2 = (idx % 2) * 512

                    @blk.scalar
                    def _(c, tt=tt, ci=ci, so2=so2, idx=idx):
                        c = _Ch(c, sc)
                        dve.wait_at(c, x1_dve)
                        if idx == 0:
                            pe.wait_at(c, pe_base2)
                        else:
                            pe.wait_at(c, stat_mm.get(idx - 2, 0))
                        sc.inc(c.activation(
                            sqs2[:, so2:so2 + 512],
                            x1sb[:, ci * OWN + tt * 512:
                                 ci * OWN + (tt + 1) * 512], AFT.Square))
                    sqv = sc.n

                    @blk.tensor
                    def _(t, tt=tt, ci=ci, so2=so2, sqv=sqv, idx=idx):
                        if idx == 0:
                            dve.wait_at(t, x1_dve)
                        sc.wait_at(t, sqv)
                        t.matmul(psY[0:1, tt * 512:(tt + 1) * 512],
                                 onesf[:],
                                 x1sb[:, ci * OWN + tt * 512:
                                      ci * OWN + (tt + 1) * 512],
                                 start=(ci == 0), stop=(ci == CI - 1))
                        mm = t.matmul(
                            psY[0:1, 1024 + tt * 512:1536 + tt * 512],
                            onesf[:], sqs2[:, so2:so2 + 512],
                            start=(ci == 0), stop=(ci == CI - 1))
                        pe.inc(mm)
                    stat_mm[idx] = pe.n
                ln2s_v = pe.n

                @blk.vector
                def _(v):
                    v = _Ch(v, dve)
                    pe.wait_at(v, ln2s_v)
                    v.tensor_scalar_mul(sqs2[0:1, :], psY[0:1, 0:OWN],
                                        1.0 / D)
                    v.tensor_mul(x0tm2[0][0:1, 0:1024], sqs2[0:1, :],
                                 sqs2[0:1, :])
                    dve.inc(v.scalar_tensor_tensor(
                        x0tm2[0][0:1, 0:1024], psY[0:1, OWN:2 * OWN],
                        1.0 / D, x0tm2[0][0:1, 0:1024],
                        AOP.mult, AOP.subtract))
                var2_v = dve.n

                @blk.scalar
                def _(c):
                    c = _Ch(c, sc)
                    dve.wait_at(c, var2_v)
                    sc.inc(c.activation(x0tm2[0][0:1, 0:1024],
                                        x0tm2[0][0:1, 0:1024], AFT.Sqrt,
                                        bias=epsc[0:1, 0:1]))
                sd2_v = sc.n

                @blk.vector
                def _(v):
                    v = _Ch(v, dve)
                    sc.wait_at(v, sd2_v)
                    dve.inc(v.reciprocal(x0tm2[0][0:1, 0:1024],
                                         x0tm2[0][0:1, 0:1024]))
                r2_v = dve.n

                @blk.tensor
                def _(t):
                    dve.wait_at(t, r2_v)
                    for sl in range(2):
                        t.matmul(psY[:, sl * 512:(sl + 1) * 512], ones1r[:],
                                 sqs2[0:1, sl * 512:(sl + 1) * 512],
                                 start=True, stop=True)
                        mm = t.matmul(
                            psY[:, 1024 + sl * 512:1536 + sl * 512],
                            ones1r[:],
                            x0tm2[0][0:1, sl * 512:(sl + 1) * 512],
                            start=True, stop=True)
                    pe.inc(mm)
                rep2_v = pe.n

                for ci in range(CI):
                    @blk.vector
                    def _(v, ci=ci):
                        v = _Ch(v, dve)
                        pe.wait_at(v, rep2_v)
                        sl_ = x1sb[:, ci * OWN:(ci + 1) * OWN]
                        hs = h1sb[:, ci * OWN:(ci + 1) * OWN]
                        v.tensor_sub(hs, sl_, psY[:, 0:OWN])
                        dve.inc(v.tensor_mul(hs, hs, psY[:, 1024:2048]))
                h1_v = dve.n
                if debug:
                    @blk.gpsimd
                    def _(g):
                        dve.wait_at(g, h1_v)
                        dbgs.inc(g.dma_start(out=dbg["dbg_h1"],
                                              in_=h1sb[:]), 16)

                @blk.tensor
                def _(t):
                    dve.wait_at(t, x1c_v)
                    sc.wait_at(t, ytail_ev)
                    pe.inc(t.matmul(psA[0:1, 0:CI], onesf[:], x1col[:],
                                    start=True, stop=True))
                cs_v = pe.n

                @blk.scalar
                def _(c):
                    c = _Ch(c, sc)
                    dve.wait_at(c, x1c_v)
                    sc.inc(c.activation(xnc[:], x1col[:], AFT.Square))
                csq_v = sc.n

                @blk.tensor
                def _(t):
                    sc.wait_at(t, csq_v)
                    sc.wait_at(t, y_ev)
                    pe.inc(t.matmul(psY[0:1, 2048:2048 + CI], onesf[:],
                                    xnc[:], start=True, stop=True))
                csqs_v = pe.n

                @blk.vector
                def _(v):
                    v = _Ch(v, dve)
                    pe.wait_at(v, csqs_v)
                    v.reduce_sum(tsc[:, 0:1], psA[0:1, 0:CI], axis=AXX)
                    v.reduce_sum(tsc[:, 1:2], psY[0:1, 2048:2048 + CI],
                                 axis=AXX)
                    v.tensor_scalar_mul(tsc[:, 0:1], tsc[:, 0:1], 1.0 / D)
                    v.tensor_scalar_mul(tsc[:, 1:2], tsc[:, 1:2], 1.0 / D)
                    v.tensor_mul(tsc[:, 2:3], tsc[:, 0:1], tsc[:, 0:1])
                    dve.inc(v.tensor_sub(tsc[:, 2:3], tsc[:, 1:2],
                                         tsc[:, 2:3]))
                cvar_v = dve.n

                @blk.scalar
                def _(c):
                    c = _Ch(c, sc)
                    dve.wait_at(c, cvar_v)
                    sc.inc(c.activation(tsc[:, 3:4], tsc[:, 2:3], AFT.Sqrt,
                                        bias=epsc[0:1, 0:1]))
                csd_v = sc.n

                @blk.vector
                def _(v):
                    v = _Ch(v, dve)
                    sc.wait_at(v, csd_v)
                    v.reciprocal(tsc[:, 4:5], tsc[:, 3:4])
                    v.tensor_mul(tsc[:, 5:6], tsc[:, 4:5], tsc[:, 4:5])
                    dve.inc(v.tensor_mul(tsc[:, 5:6], tsc[:, 2:3],
                                         tsc[:, 5:6]))
                cvh_v = dve.n

                @blk.scalar
                def _(c):
                    c = _Ch(c, sc)
                    dve.wait_at(c, cvh_v)
                    sc.inc(c.activation(tsc[:, 6:7], tsc[:, 5:6], AFT.Sqrt,
                                        bias=epsc[0:1, 1:2]))
                cvsd_v = sc.n

                @blk.vector
                def _(v):
                    v = _Ch(v, dve)
                    sc.wait_at(v, cvsd_v)
                    v.reciprocal(tsc[:, 7:8], tsc[:, 6:7])
                    dve.inc(v.tensor_mul(tsc[:, 7:8], tsc[:, 4:5],
                                         tsc[:, 7:8]))
                s_v = dve.n

                @blk.tensor
                def _(t):
                    dve.wait_at(t, s_v)
                    t.matmul(psA[:, 64:65], ones1r[:], tsc[0:1, 0:1],
                             start=True, stop=False)
                    pe.inc(t.matmul(psA[:, 65:66], ones1r[:], tsc[0:1, 7:8],
                                    start=False, stop=True))
                srep_v = pe.n

                @blk.scalar
                def _(c):
                    c = _Ch(c, sc)
                    pe.wait_at(c, srep_v)
                    sc.inc(c.copy(colsc[:], psA[:, 64:66]))
                srep_ev = sc.n

                @blk.vector
                def _(v):
                    v = _Ch(v, dve)
                    sc.wait_at(v, srep_ev)
                    ldm.wait(v)
                    v.tensor_scalar(xnc[:], x1col[:], colsc[:, 0:1],
                                    colsc[:, 1:2], AOP.subtract, AOP.mult)
                    v.tensor_mul(xnc[:], xnc[:], colc["ing"][:])
                    dve.inc(v.tensor_add(xnc[:], xnc[:], colc["inb"][:]))
                xnc_v = dve.n

            with ExitStack() as ph:
                pe_ = ph.enter_context
                th4 = pe_(nc.sbuf_tensor("th4", [4, D], F32))
                tn4 = pe_(nc.sbuf_tensor("tn4", [4, D], F32))
                t14 = pe_(nc.sbuf_tensor("t14", [4, 1024], F32))
                tstat = pe_(nc.sbuf_tensor("tstat", [4, 8], F32))
                tcol4 = pe_(nc.sbuf_tensor("tcol4", [128, 64], F32))
                arb = pe_(nc.sbuf_tensor("arb", [4, D], F32))
                lwb = [pe_(nc.sbuf_tensor(f"lwb{i}", [128, 2048], F32))
                       for i in range(3)]
                rowv = {nm: pe_(nc.sbuf_tensor(nm + "_s", [4, w], F32))
                        for nm, w in (("png", D), ("pnb", D), ("aggb", D),
                                      ("bcb", D), ("d1b", 1024), ("d2b", D))}
                ohh = pe_(nc.sbuf_tensor("ohh_s", [1, 4], F32))
                oh1 = pe_(nc.sbuf_tensor("oh1_s", [4, 1], F32))
                psL = pe_(nc.psum_tensor("psL", [4, 2048], F32))
                psC = pe_(nc.psum_tensor("psC", [128, 64], F32))

                @blk.sync
                def _(q):
                    dve.wait_at(q, dve.n)
                    sc.wait(q)
                    pe.wait(q)
                    gq.wait(q)
                    spl.wait(q)
                    dbgs.wait(q)
                    for dst, src in [(rowv["png"], png_i), (rowv["pnb"], pnb_i),
                                     (rowv["aggb"], aggb_i),
                                     (rowv["bcb"], bcb_i),
                                     (rowv["d1b"], d1b_i),
                                     (rowv["d2b"], d2b_i),
                                     (ohh, ohh_i), (oh1, oh1_i)]:
                        ldm.inc(q.dma_start(out=dst[:], in_=src), 16)
                ldm4 = ldm.n

                lw_ld = {}
                lw_i = [0]
                ldw3 = [S["ldwa"], S["ldwb"], S["ldwc"]]
                lsrc = [agg_t[ci].transpose([1, 0, 2]) for ci in range(CI)]
                for it in range(4):
                    lsrc += [d1s_t[z].transpose([1, 0, 2]) for z in range(8)]
                    lsrc += [d2s_t[z].transpose([1, 0, 2]) for z in range(8)]
                lsrc += [bc_t[ci].transpose([1, 0, 2]) for ci in range(CI)]

                def stream_lw(q):
                    i = lw_i[0]
                    ldw3[i % 3].inc(
                        q.dma_start(out=lwb[i % 3][:]
                                    .rearrange("p (a b) -> p a b", a=4),
                                    in_=lsrc[i]), 16)
                    lw_ld[i] = (i % 3, ldw3[i % 3].n)
                    lw_i[0] += 1

                def lw_next(q):
                    if lw_i[0] < len(lsrc):
                        pe.wait_at(q, pe.n)
                        stream_lw(q)

                @blk.sync
                def _(q):
                    dve.wait_at(q, xnc_v)
                    ldw[0].wait_at(q, ldw[0].n)
                    ldw[1].wait_at(q, ldw[1].n)
                    stream_lw(q)
                    stream_lw(q)
                    stream_lw(q)

                @blk.tensor
                def _(t):
                    dve.wait_at(t, xnc_v)
                    sc.wait_at(t, sc.n)
                    ldm.wait_at(t, ldm4)
                    for i in range(CI):
                        mm = t.matmul(psL[0:1, i * 128:(i + 1) * 128],
                                      xnc[:, i:i + 1], ident[:],
                                      is_transpose=True,
                                      start=(i % 4 == 0),
                                      stop=(i % 4 == 3))
                    pe.inc(mm)
                xr_v = pe.n

                @blk.scalar
                def _(c):
                    c = _Ch(c, sc)
                    pe.wait_at(c, xr_v)
                    sc.inc(c.copy(arb[0:1, :], psL[0:1, :]))
                xr_ev = sc.n

                @blk.tensor
                def _(t):
                    sc.wait_at(t, xr_ev)
                    for sl in range(4):
                        mm = t.matmul(psL[:, sl * 512:(sl + 1) * 512],
                                      ohh[:], arb[0:1, sl * 512:(sl + 1) * 512],
                                      start=True, stop=True)
                    pe.inc(mm)
                ai_v = pe.n

                @blk.scalar
                def _(c):
                    c = _Ch(c, sc)
                    pe.wait_at(c, ai_v)
                    sc.inc(c.copy(arb[:], psL[:]))
                ai_ev = sc.n

                CC = S["cc"]
                CCD = S["ccd"]

                def do_allreduce(wait_sc_v):
                    @blk.gpsimd
                    def _(g, wait_sc_v=wait_sc_v):
                        sc.wait_at(g, wait_sc_v)
                        CCD.inc(g.dma_start(out=arin, in_=arb[:]), 16)
                        g.wait_ge(CCD.h, CCD.n)
                        CC.inc(g.collective_compute(
                            "AllReduce", AOP.add,
                            replica_groups=[list(range(8))],
                            ins=[arin.opt()], outs=[arout.opt()]))
                        g.wait_ge(CC.h, CC.n)
                        CCD.inc(g.dma_start(out=arb[:], in_=arout), 16)
                    return CCD.n

                ar_v = do_allreduce(ai_ev)

                def vec2col4(src, n, wait_pe_list, extra=0):
                    @blk.tensor
                    def _(t):
                        for w in wait_pe_list:
                            w(t)
                        for i in range(n):
                            mm = t.matmul(psC[:, i * 4:(i + 1) * 4],
                                          src[0:4, i * 128:(i + 1) * 128],
                                          ident[0:4, 0:4], is_transpose=True,
                                          start=(i == 0), stop=(i == n - 1))
                        pe.inc(mm)
                    vv = pe.n

                    @blk.scalar
                    def _(c, vv=vv, n=n):
                        c = _Ch(c, sc)
                        pe.wait_at(c, vv)
                        sc.inc(c.copy(tcol4[:, 0:n * 4], psC[:, 0:n * 4]))
                    return sc.n

                tc_v = vec2col4(arb, CI,
                                [lambda t: CCD.wait_at(t, ar_v),
                                 lambda t: sc.wait_at(t, sc.n)])

                for ci in range(CI):
                    @blk.tensor
                    def _(t, ci=ci):
                        bi, v = lw_ld[ci]
                        ldw3[bi].wait_at(t, v)
                        sc.wait_at(t, tc_v)
                        for ft in range(4):
                            mm = t.matmul(
                                psL[:, ft * 512:(ft + 1) * 512],
                                tcol4[:, ci * 4:(ci + 1) * 4],
                                lwb[ci % 3][:, ft * 512:(ft + 1) * 512],
                                start=(ci == 0), stop=(ci == CI - 1))
                        pe.inc(mm)

                    @blk.sync
                    def _(q):
                        lw_next(q)
                agg_v = pe.n

                @blk.vector
                def _(v):
                    v = _Ch(v, dve)
                    pe.wait_at(v, agg_v)
                    ldm.wait_at(v, ldm4)
                    dve.inc(v.tensor_add(th4[:], psL[:], rowv["aggb"][:]))
                th_v = dve.n

                for it in range(4):
                    @blk.vector
                    def _(v):
                        v = _Ch(v, dve)
                        dve.wait_at(v, dve.n)
                        dve.inc(v.reduce_sum(tstat[:, 0:1], th4[:], axis=AXX))
                    ts_v = dve.n

                    @blk.scalar
                    def _(c, ts_v=ts_v):
                        c = _Ch(c, sc)
                        dve.wait_at(c, ts_v)
                        sc.inc(c.activation(tn4[:], th4[:], AFT.Square,
                                            accum_out=tstat[:, 1:2]))
                    tsq_v = sc.n

                    @blk.vector
                    def _(v, tsq_v=tsq_v):
                        v = _Ch(v, dve)
                        sc.wait_at(v, tsq_v)
                        v.tensor_scalar_mul(tstat[:, 0:1], tstat[:, 0:1],
                                            1.0 / D)
                        v.tensor_scalar_mul(tstat[:, 1:2], tstat[:, 1:2],
                                            1.0 / D)
                        v.tensor_mul(tstat[:, 2:3], tstat[:, 0:1],
                                     tstat[:, 0:1])
                        dve.inc(v.tensor_sub(tstat[:, 2:3], tstat[:, 1:2],
                                             tstat[:, 2:3]))
                    tv_v = dve.n

                    @blk.scalar
                    def _(c, tv_v=tv_v):
                        c = _Ch(c, sc)
                        dve.wait_at(c, tv_v)
                        sc.inc(c.activation(tstat[:, 3:4], tstat[:, 2:3],
                                            AFT.Sqrt, bias=epsc[0:4, 1:2]))
                    tsd_v = sc.n

                    @blk.vector
                    def _(v, tsd_v=tsd_v):
                        v = _Ch(v, dve)
                        sc.wait_at(v, tsd_v)
                        v.reciprocal(tstat[:, 4:5], tstat[:, 3:4])
                        v.tensor_scalar(tn4[:], th4[:], tstat[:, 0:1],
                                        tstat[:, 4:5], AOP.subtract, AOP.mult)
                        v.tensor_mul(tn4[:], tn4[:], rowv["png"][:])
                        dve.inc(v.tensor_add(tn4[:], tn4[:], rowv["pnb"][:]))
                    tn_v = dve.n

                    tcv = vec2col4(tn4, CI,
                                   [lambda t, tn_v=tn_v: dve.wait_at(t, tn_v),
                                    lambda t: sc.wait_at(t, sc.n)])

                    for z in range(8):
                        ht, cg = divmod(z, 4)
                        li = CI + it * 16 + z

                        @blk.tensor
                        def _(t, z=z, ht=ht, cg=cg, li=li, tcv=tcv):
                            bi, v = lw_ld[li]
                            ldw3[bi].wait_at(t, v)
                            sc.wait_at(t, tcv)
                            if z == 0:
                                dve.wait_at(t, dve.n)
                            for c4 in range(4):
                                ci = cg * 4 + c4
                                mm = t.matmul(
                                    psL[:, ht * 512:(ht + 1) * 512],
                                    tcol4[:, ci * 4:(ci + 1) * 4],
                                    lwb[li % 3][:, c4 * 512:(c4 + 1) * 512],
                                    start=(ci == 0), stop=(ci == CI - 1))
                            pe.inc(mm)

                        @blk.sync
                        def _(q):
                            lw_next(q)
                    d1_v = pe.n

                    @blk.scalar
                    def _(c, d1_v=d1_v):
                        c = _Ch(c, sc)
                        pe.wait_at(c, d1_v)
                        sc.inc(c.copy(t14[:], psL[:, 0:1024]))
                    d1_ev = sc.n

                    @blk.vector
                    def _(v, d1_ev=d1_ev):
                        v = _Ch(v, dve)
                        sc.wait_at(v, d1_ev)
                        dve.inc(v.tensor_add(t14[:], t14[:], rowv["d1b"][:]))
                    d1b_v = dve.n

                    @blk.scalar
                    def _(c, d1b_v=d1b_v):
                        c = _Ch(c, sc)
                        dve.wait_at(c, d1b_v)
                        sc.inc(c.activation(t14[:], t14[:], AFT.Gelu))
                    gel_v = sc.n

                    t1cv = vec2col4(t14, 8,
                                    [lambda t, gel_v=gel_v: sc.wait_at(t, gel_v)])

                    for hc in range(8):
                        li = CI + it * 16 + 8 + hc

                        @blk.tensor
                        def _(t, hc=hc, li=li, t1cv=t1cv):
                            bi, v = lw_ld[li]
                            ldw3[bi].wait_at(t, v)
                            sc.wait_at(t, t1cv)
                            if hc == 0:
                                dve.wait_at(t, dve.n)
                            for ft in range(4):
                                mm = t.matmul(
                                    psL[:, ft * 512:(ft + 1) * 512],
                                    tcol4[:, hc * 4:(hc + 1) * 4],
                                    lwb[li % 3][:, ft * 512:(ft + 1) * 512],
                                    start=(hc == 0), stop=(hc == 7))
                            pe.inc(mm)

                        @blk.sync
                        def _(q):
                            lw_next(q)
                    d2_v = pe.n

                    @blk.scalar
                    def _(c, d2_v=d2_v):
                        c = _Ch(c, sc)
                        pe.wait_at(c, d2_v)
                        sc.inc(c.copy(arb[:], psL[:]))
                    d2_ev = sc.n

                    ar2_v = do_allreduce(d2_ev)

                    @blk.vector
                    def _(v, ar2_v=ar2_v):
                        v = _Ch(v, dve)
                        CCD.wait_at(v, ar2_v)
                        v.tensor_add(th4[:], th4[:], arb[:])
                        dve.inc(v.tensor_add(th4[:], th4[:], rowv["d2b"][:]))

                thc_v = vec2col4(th4, CI,
                                 [lambda t: dve.wait_at(t, dve.n),
                                  lambda t: sc.wait_at(t, sc.n)])

                for ci in range(CI):
                    li = CI + 64 + ci

                    @blk.tensor
                    def _(t, ci=ci, li=li):
                        bi, v = lw_ld[li]
                        ldw3[bi].wait_at(t, v)
                        sc.wait_at(t, thc_v)
                        if ci == 0:
                            dve.wait_at(t, dve.n)
                        for ft in range(4):
                            mm = t.matmul(
                                psL[:, ft * 512:(ft + 1) * 512],
                                tcol4[:, ci * 4:(ci + 1) * 4],
                                lwb[li % 3][:, ft * 512:(ft + 1) * 512],
                                start=(ci == 0), stop=(ci == CI - 1))
                        pe.inc(mm)

                    @blk.sync
                    def _(q):
                        lw_next(q)
                bc_v = pe.n

                @blk.vector
                def _(v):
                    v = _Ch(v, dve)
                    pe.wait_at(v, bc_v)
                    dve.inc(v.tensor_add(arb[:], psL[:], rowv["bcb"][:]))
                bcv_v = dve.n

                @blk.tensor
                def _(t):
                    dve.wait_at(t, bcv_v)
                    for sl in range(4):
                        mm = t.matmul(psL[0:1, sl * 512:(sl + 1) * 512],
                                      oh1[:], arb[:, sl * 512:(sl + 1) * 512],
                                      start=True, stop=True)
                    pe.inc(mm)
                bo_v = pe.n

                @blk.scalar
                def _(c):
                    c = _Ch(c, sc)
                    pe.wait_at(c, bo_v)
                    sc.inc(c.copy(tn4[0:1, :], psL[0:1, :]))
                bo_ev = sc.n

                @blk.tensor
                def _(t):
                    sc.wait_at(t, bo_ev)
                    for i in range(CI):
                        mm = t.matmul(psC[:, i:i + 1],
                                      tn4[0:1, i * 128:(i + 1) * 128],
                                      ident[0:1, 0:1], is_transpose=True,
                                      start=(i == 0), stop=(i == 15))
                    pe.inc(mm)
                bcc_v = pe.n

                @blk.scalar
                def _(c):
                    c = _Ch(c, sc)
                    pe.wait_at(c, bcc_v)
                    sc.inc(c.copy(bccol[:], psC[:, 0:16]))
                bccol_v = sc.n

                if debug:
                    @blk.gpsimd
                    def _(g):
                        sc.wait_at(g, bccol_v)
                        dbgs.inc(g.dma_start(out=dbg["dbg_th"],
                                             in_=th4[0:1, :]), 16)
                        dbgs.inc(g.dma_start(out=dbg["dbg_bc"],
                                             in_=bccol[:]), 16)

            with ExitStack() as ph:
                pe_ = ph.enter_context
                gwb = [pe_(nc.sbuf_tensor(f"gwb{i}", [128, CI * 128], F32))
                       for i in range(2)]
                gatesb = pe_(nc.sbuf_tensor("gatesb", [128, CI * OWN], F32))
                sqs3 = pe_(nc.sbuf_tensor("sqs3", [128, 1024], F32))
                st3 = pe_(nc.sbuf_tensor("st3", [1, 3 * OWN], F32))
                musc3 = pe_(nc.sbuf_tensor("musc3", [128, OWN], F32))
                rsc3 = pe_(nc.sbuf_tensor("rsc3", [128, OWN], F32))
                x1rb = [pe_(nc.sbuf_tensor(f"x1rb{i}", [128, OWN], F32))
                        for i in range(4)]
                outtm = pe_(nc.sbuf_tensor("outtm", [128, D], F32))
                psV1b = pe_(nc.psum_tensor("psV1b", [128, 2048], F32))
                psG = [pe_(nc.psum_tensor(f"psG{i}", [128, 512], F32))
                       for i in range(2)]

                g_ld = {}

                def load_g(q, co):
                    src = gate_t[co].transpose([1, 0, 2])
                    ldw[co % 2].inc(
                        q.dma_start(out=gwb[co % 2][:]
                                    .rearrange("p (a b) -> p a b", a=CI),
                                    in_=src), 16)
                    g_ld[co] = ldw[co % 2].n

                @blk.sync
                def _(q):
                    sc.wait(q)
                    dve.wait(q)
                    pe.wait(q)
                    dbgs.wait(q)
                    ldw[0].wait_at(q, ldw[0].n)
                    ldw[1].wait_at(q, ldw[1].n)
                    load_g(q, 0)
                    load_g(q, 1)

                gev = {}
                for co in range(CI):
                    @blk.sync
                    def _(q, co=co):
                        if co >= 2:
                            pe.wait_at(q, pe.n)
                            load_g(q, co)
                    for tt in range(2):
                        pg = (co * 2 + tt) % 2

                        @blk.tensor
                        def _(t, co=co, tt=tt, pg=pg):
                            ldw[co % 2].wait_at(t, g_ld[co])
                            dve.wait_at(t, h1_v)
                            sc.wait_at(t, gev.get((co * 2 + tt) - 2, 0))
                            for ci in range(CI):
                                mm = t.matmul(
                                    psG[pg][:],
                                    gwb[co % 2][:, ci * 128:(ci + 1) * 128],
                                    h1sb[:, ci * OWN + tt * 512:
                                         ci * OWN + (tt + 1) * 512],
                                    start=(ci == 0), stop=(ci == CI - 1))
                            pe.inc(mm)
                        gmm = pe.n

                        @blk.scalar
                        def _(c, co=co, tt=tt, pg=pg, gmm=gmm):
                            c = _Ch(c, sc)
                            pe.wait_at(c, gmm)
                            sc.inc(c.activation(
                                gatesb[:, co * OWN + tt * 512:
                                       co * OWN + (tt + 1) * 512],
                                psG[pg][:], AFT.Sigmoid,
                                bias=colc["gb"][:, co:co + 1]))
                        gev[co * 2 + tt] = sc.n
                gate_v = sc.n
                x1r_ld = {}

                @blk.gpsimd
                def _(g):
                    spl.wait_at(g, x1_spl)
                    dve.wait_at(g, dve.n)
                    sc.wait_at(g, sc.n)
                    pe.wait_at(g, pe.n)
                    for ci in (0, 2):
                        ldx1[0].inc(g.dma_start(
                            out=x1rb[ci][:],
                            in_=x1_sp[:, ci * OWN:(ci + 1) * OWN]), 16)
                        x1r_ld[ci] = ldx1[0].n

                @blk.sync
                def _(q):
                    spl.wait_at(q, x1_spl)
                    dve.wait_at(q, dve.n)
                    sc.wait_at(q, sc.n)
                    pe.wait_at(q, pe.n)
                    for ci in (1, 3):
                        ldx1[1].inc(q.dma_start(
                            out=x1rb[ci][:],
                            in_=x1_sp[:, ci * OWN:(ci + 1) * OWN]), 16)
                        x1r_ld[ci] = ldx1[1].n
                dbgg_v = dbgs.n
                if debug:
                    @blk.gpsimd
                    def _(g):
                        sc.wait_at(g, gate_v)
                        dbgs.inc(g.dma_start(out=dbg["dbg_gate"],
                                              in_=gatesb[:]), 16)
                    dbgg_v = dbgs.n

                for ci in range(CI):
                    @blk.vector
                    def _(v, ci=ci):
                        v = _Ch(v, dve)
                        sc.wait_at(v, gate_v)
                        sc.wait_at(v, bccol_v)
                        dbgs.wait_at(v, dbgg_v)
                        sl_ = gatesb[:, ci * OWN:(ci + 1) * OWN]
                        dve.inc(v.scalar_tensor_tensor(
                            sl_, sl_, bccol[:, ci:ci + 1],
                            h1sb[:, ci * OWN:(ci + 1) * OWN],
                            AOP.mult, AOP.add))
                of_v = dve.n

                stat_mm3 = {}
                pe_base3 = pe.n
                for idx in range(2 * CI):
                    tt, ci = divmod(idx, CI)
                    so3 = (idx % 2) * 512

                    @blk.scalar
                    def _(c, tt=tt, ci=ci, so3=so3, idx=idx):
                        c = _Ch(c, sc)
                        dve.wait_at(c, of_v)
                        if idx == 0:
                            pe.wait_at(c, pe_base3)
                        else:
                            pe.wait_at(c, stat_mm3.get(idx - 2, 0))
                        sc.inc(c.activation(
                            sqs3[:, so3:so3 + 512],
                            gatesb[:, ci * OWN + tt * 512:
                                   ci * OWN + (tt + 1) * 512],
                            AFT.Square))
                    sq3v = sc.n

                    @blk.tensor
                    def _(t, tt=tt, ci=ci, so3=so3, sq3v=sq3v, idx=idx):
                        if idx == 0:
                            dve.wait_at(t, of_v)
                        sc.wait_at(t, sq3v)
                        t.matmul(psV1b[0:1, tt * 512:(tt + 1) * 512],
                                 onesf[:],
                                 gatesb[:, ci * OWN + tt * 512:
                                        ci * OWN + (tt + 1) * 512],
                                 start=(ci == 0), stop=(ci == CI - 1))
                        mm = t.matmul(
                            psV1b[0:1, 1024 + tt * 512:1536 + tt * 512],
                            onesf[:], sqs3[:, so3:so3 + 512],
                            start=(ci == 0), stop=(ci == CI - 1))
                        pe.inc(mm)
                    stat_mm3[idx] = pe.n
                ol_v = pe.n

                @blk.vector
                def _(v):
                    v = _Ch(v, dve)
                    pe.wait_at(v, ol_v)
                    v.tensor_scalar_mul(st3[0:1, 0:OWN], psV1b[0:1, 0:OWN],
                                        1.0 / D)
                    v.tensor_scalar_mul(st3[0:1, 2 * OWN:3 * OWN],
                                        psV1b[0:1, OWN:2 * OWN], 1.0 / D)
                    v.tensor_mul(st3[0:1, OWN:2 * OWN], st3[0:1, 0:OWN],
                                 st3[0:1, 0:OWN])
                    dve.inc(v.tensor_sub(st3[0:1, OWN:2 * OWN],
                                         st3[0:1, 2 * OWN:3 * OWN],
                                         st3[0:1, OWN:2 * OWN]))
                var3_v = dve.n

                @blk.scalar
                def _(c):
                    c = _Ch(c, sc)
                    dve.wait_at(c, var3_v)
                    sc.inc(c.activation(st3[0:1, OWN:2 * OWN],
                                        st3[0:1, OWN:2 * OWN], AFT.Sqrt,
                                        bias=epsc[0:1, 1:2]))
                sd3_v = sc.n

                @blk.vector
                def _(v):
                    v = _Ch(v, dve)
                    sc.wait_at(v, sd3_v)
                    dve.inc(v.reciprocal(st3[0:1, OWN:2 * OWN],
                                         st3[0:1, OWN:2 * OWN]))
                r3_v = dve.n

                @blk.tensor
                def _(t):
                    dve.wait_at(t, r3_v)
                    for sl in range(2):
                        t.matmul(psV1b[:, sl * 512:(sl + 1) * 512], ones1r[:],
                                 st3[0:1, sl * 512:(sl + 1) * 512],
                                 start=True, stop=True)
                        mm = t.matmul(
                            psV1b[:, 1024 + sl * 512:1536 + sl * 512],
                            ones1r[:],
                            st3[0:1, OWN + sl * 512:OWN + (sl + 1) * 512],
                            start=True, stop=True)
                    pe.inc(mm)
                rep3_v = pe.n

                @blk.scalar
                def _(c):
                    c = _Ch(c, sc)
                    pe.wait_at(c, rep3_v)
                    c.copy(musc3[:], psV1b[:, 0:OWN])
                    sc.inc(c.copy(rsc3[:], psV1b[:, 1024:2048]))
                rep3_ev = sc.n

                x2sb = h1sb
                for ci in range(CI):
                    @blk.vector
                    def _(v, ci=ci):
                        v = _Ch(v, dve)
                        sc.wait_at(v, rep3_ev)
                        sl_ = gatesb[:, ci * OWN:(ci + 1) * OWN]
                        v.tensor_sub(sl_, sl_, musc3[:])
                        dve.inc(v.tensor_mul(sl_, sl_, rsc3[:]))
                    n3v = dve.n

                    @blk.vector
                    def _(v, ci=ci, n3v=n3v):
                        v = _Ch(v, dve)
                        dve.wait_at(v, n3v)
                        v.tensor_scalar(
                            x2sb[:, ci * OWN:(ci + 1) * OWN],
                            gatesb[:, ci * OWN:(ci + 1) * OWN],
                            colc["og"][:, ci:ci + 1], colc["ob"][:, ci:ci + 1],
                            AOP.mult, AOP.add)
                        ldx1[ci % 2].wait(v)
                        sl_ = x2sb[:, ci * OWN:(ci + 1) * OWN]
                        dve.inc(v.tensor_add(sl_, sl_, x1rb[ci % 4][:]))
                    addv = dve.n

                    @blk.gpsimd
                    def _(g, ci=ci, addv=addv):
                        if ci + 4 < CI and ci % 2 == 0:
                            dve.wait_at(g, addv)
                            ldx1[0].inc(g.dma_start(
                                out=x1rb[ci % 4][:],
                                in_=x1_sp[:, (ci + 4) * OWN:(ci + 5) * OWN]),
                                16)
                            x1r_ld[ci + 4] = ldx1[0].n

                    @blk.sync
                    def _(q, ci=ci, addv=addv):
                        if ci + 4 < CI and ci % 2 == 1:
                            dve.wait_at(q, addv)
                            ldx1[1].inc(q.dma_start(
                                out=x1rb[ci % 4][:],
                                in_=x1_sp[:, (ci + 4) * OWN:(ci + 5) * OWN]),
                                16)
                            x1r_ld[ci + 4] = ldx1[1].n
                x2_v = dve.n

                out_dma = 0
                for tc in range(8):
                    for gg in range(4):
                        @blk.tensor
                        def _(t, tc=tc, gg=gg, out_dma=out_dma):
                            dve.wait_at(t, x2_v)
                            sc.wait_at(t, sc.n)
                            if gg == 0:
                                out_s.wait_at(t, out_dma)
                            for i4 in range(4):
                                ci = gg * 4 + i4
                                mm = t.matmul(
                                    psG[0][:, i4 * 128:(i4 + 1) * 128],
                                    x2sb[:, ci * OWN + tc * 128:
                                         ci * OWN + (tc + 1) * 128],
                                    ident[:], is_transpose=True,
                                    start=(i4 == 0), stop=(i4 == 3))
                            pe.inc(mm)
                        ftr = pe.n

                        @blk.scalar
                        def _(c, tc=tc, gg=gg, ftr=ftr):
                            c = _Ch(c, sc)
                            pe.wait_at(c, ftr)
                            dst = outtm[:].rearrange(
                                "p (ci t) -> p ci t", ci=CI)[
                                :, gg * 4:(gg + 1) * 4, 0:128]
                            sc.inc(c.copy(dst, psG[0][:]
                                          .rearrange("p (a b) -> p a b",
                                                     a=4)))

                    @blk.scalar
                    def _(c, tc=tc):
                        c = _Ch(c, sc)
                        sc.wait_at(c, sc.n)
                        out_s.inc(c.dma_start(
                            out=y_out[tc * 128:(tc + 1) * 128, :],
                            in_=outtm[:]), 16)
                    out_dma = out_s.n

                @blk.scalar
                def _(c):
                    c = _Ch(c, sc)
                    out_s.wait_at(c, out_s.n)
                    gq.wait_at(c, gq.n)
                    spl.wait_at(c, spl.n)
                    dbgs.wait_at(c, dbgs.n)
                    S["ccd"].wait_at(c, S["ccd"].n)
    return nc


_PROG_CACHE = {}


def _get_prog(debug=False):
    key = bool(debug)
    if key not in _PROG_CACHE:
        _PROG_CACHE[key] = _build(key)
    return _PROG_CACHE[key]


def _host_prep(inputs):
    x = np.ascontiguousarray(np.asarray(inputs["L0_attn_x"], dtype=np.float32))
    wqkv = np.asarray(inputs["L0_attn_w_qkv"], dtype=np.float32)
    wout = np.asarray(inputs["L0_attn_w_out"], dtype=np.float32)
    gw = np.asarray(inputs["L0_ffn_gate_w"], dtype=np.float32)
    aggw = np.asarray(inputs["L0_ffn_agg_w"], dtype=np.float32)
    bcw = np.asarray(inputs["L0_ffn_bc_w"], dtype=np.float32)
    d1w = np.asarray(inputs["L0_ffn_d1_w"], dtype=np.float32)
    d2w = np.asarray(inputs["L0_ffn_d2_w"], dtype=np.float32)

    wq_t = np.ascontiguousarray(
        wqkv.T.reshape(CI, 128, 3, NH, 128).transpose(3, 2, 0, 1, 4)
        .reshape(3 * NH, CI, 128, 128))
    W = np.ascontiguousarray(wout.T)
    wout_t = np.ascontiguousarray(
        W.reshape(NH, 128, CI, 128).transpose(0, 2, 1, 3))
    gate_t = np.ascontiguousarray(
        gw.reshape(CI, 128, CI, 128).transpose(2, 0, 1, 3))
    agg_t = np.ascontiguousarray(
        aggw.reshape(CI, 128, 4, 512).transpose(0, 2, 1, 3))
    bc_t = np.ascontiguousarray(
        bcw.reshape(CI, 128, 4, 512).transpose(0, 2, 1, 3))


    def colform(v):
        return np.ascontiguousarray(
            np.asarray(v, np.float32).reshape(CI, 128).T)

    def rowform(v):
        return np.ascontiguousarray(np.asarray(v, np.float32)[None, :])

    def rowform4(v):
        return np.ascontiguousarray(
            np.broadcast_to(np.asarray(v, np.float32)[None, :], (4, len(v)))
            .copy())

    common = {
        "wq_t": wq_t, "wout_t": wout_t, "gate_t": gate_t,
        "agg_t": agg_t, "bc_t": bc_t,
        "ident": np.eye(128, dtype=np.float32),
        "ing_c": colform(inputs["L0_ffn_in_gamma"]),
        "inb_c": colform(inputs["L0_ffn_in_beta"]),
        "og_c": colform(inputs["L0_ffn_out_gamma"]),
        "ob_c": colform(inputs["L0_ffn_out_beta"]),
        "gb_c": colform(inputs["L0_ffn_gate_b"]),
        "png_r": rowform4(inputs["L0_ffn_pn_gamma"]),
        "pnb_r": rowform4(inputs["L0_ffn_pn_beta"]),
        "aggb_r": rowform4(inputs["L0_ffn_agg_b"]),
        "bcb_r": rowform4(inputs["L0_ffn_bc_b"]),
        "d2b_r": rowform4(inputs["L0_ffn_d2_b"]),
    }

    inv = (10000.0 ** (-(np.arange(16) / 16.0))).astype(np.float64)
    maskD = np.zeros((128, 128), np.float32)
    for kk in range(128):
        maskD[kk, :kk] = NEGBIG

    in_maps, perms = [], []
    for c in range(8):
        b, par = divmod(c, 2)
        order = list(range(par, 16, 2)) + list(range(1 - par, 16, 2))
        perm = np.concatenate([np.arange(g * 128, (g + 1) * 128)
                               for g in order])
        perms.append(perm)
        x0l = np.concatenate([x[b][perm], x[b][1920:2048]], axis=0)
        posK = perm.astype(np.float64)
        angK = posK[None, :] * inv[:, None]
        posQ = np.concatenate([perm[:OWN].astype(np.float64), [2047.0]])
        angQ = posQ[None, :] * inv[:, None]
        angA = np.concatenate([angK, 2047.0 * inv[:, None]], axis=1)
        cosTt = np.zeros((32, T + 1), np.float32)
        sinTt = np.zeros((32, T + 1), np.float32)
        cosTt[0:16] = cosTt[16:32] = np.cos(angA)
        sinTt[0:16] = -np.sin(angA)
        sinTt[16:32] = np.sin(angA)
        maskO = (np.full((128, 128), NEGBIG, np.float32) if par == 0
                 else np.zeros((128, 128), np.float32))
        hs = slice(c * 1024, (c + 1) * 1024)
        d1s = np.ascontiguousarray(
            d1w[:, hs].reshape(CI, 128, 2, 512).transpose(2, 0, 1, 3)
            .reshape(2, 4, 4, 128, 512).reshape(8, 4, 128, 512))
        d2s = np.ascontiguousarray(
            d2w[hs, :].reshape(8, 128, 4, 512).transpose(0, 2, 1, 3))
        d1bs = np.ascontiguousarray(
            np.broadcast_to(np.asarray(inputs["L0_ffn_d1_b"], np.float32)
                            [None, hs], (4, 1024)).copy())
        ohh = np.zeros((1, 4), np.float32)
        ohh[0, b] = 0.5
        oh1 = np.zeros((4, 1), np.float32)
        oh1[b, 0] = 1.0
        m = dict(common)
        m.update({"x0l": np.ascontiguousarray(x0l), "cosT": cosTt,
                  "sinT": sinTt, "maskD": maskD, "maskO": maskO,
                  "d1s_t": d1s, "d2s_t": d2s, "d1b_r": d1bs,
                  "ohh_r": ohh, "oh1_r": oh1})
        in_maps.append(m)
    return in_maps, perms


def kernel(**inputs):
    in_maps, perms = _host_prep(inputs)
    nc = _get_prog(debug=False)
    res = run_bass_kernel_spmd(nc, in_maps, core_ids=list(range(8)))
    out = np.empty((4, T, D), dtype=np.float32)
    for c in range(8):
        out[c // 2][perms[c][:OWN]] = res.results[c]["y_out"]
    return out
